# version 01
# baseline (speedup 1.0000x reference)
import numpy as np
import ml_dtypes
from contextlib import ExitStack

import concourse.bass as bass
import concourse.mybir as mybir
from concourse.bass_utils import run_bass_kernel_spmd

F32 = mybir.dt.float32
BF16 = mybir.dt.bfloat16
AF = mybir.ActivationFunctionType
ALU = mybir.AluOpType
AX = mybir.AxisListType

BFD = ml_dtypes.bfloat16

D = 1024
H = 16
DH = 64
WIN = 256
S = 5888
PADT = 1792
REALT = 4096
NCH = S // 128
NAT = S // 256
DI2 = 2816
NJP = 22
NEG = -1e30


class Buf:
    __slots__ = ("w", "rs")

    def __init__(self):
        self.w = None
        self.rs = []

    def wdep(self):
        d = list(self.rs)
        if self.w is not None:
            d.append(self.w)
        return d

    def did_read(self, t):
        if t is not None:
            self.rs.append(t)

    def did_write(self, t):
        if t is not None:
            self.w = t
            self.rs = []


class Prog:
    ENG = ("sync", "scalar", "vector", "tensor", "gpsimd")
    CSEM = {"tensor": "C_t", "vector": "C_v", "scalar": "C_a", "gpsimd": "C_p"}
    NLANE = 8

    def __init__(self, nc, stack):
        self.nc = nc
        self.q = {e: [] for e in self.ENG}
        self.semh = {}
        self.cnt = {}
        self.wm = {e: {} for e in self.ENG}
        for name in ("C_t", "C_v", "C_a", "C_p"):
            self.semh[name] = stack.enter_context(nc.semaphore(name))
            self.cnt[name] = 0
        self.qidx = {}
        for qn in ("sync", "scalar", "gpsimd"):
            self.qidx[qn] = 0
            for ln in range(self.NLANE):
                name = f"Q_{qn}_{ln}"
                self.semh[name] = stack.enter_context(nc.semaphore(name))
                self.cnt[name] = 0

    def emit(self, eng, fn, deps=(), inc=True, dma=False):
        waits = []
        for d in deps:
            if d is None:
                continue
            s, v, peng = d
            if v <= 0:
                continue
            if peng == eng == "tensor":
                continue
            if self.wm[eng].get(s, 0) >= v:
                continue
            waits.append((s, v))
            self.wm[eng][s] = v
        if dma:
            i = self.qidx[eng]
            self.qidx[eng] = i + 1
            sname = f"Q_{eng}_{i % self.NLANE}"
            amt = 16
            prev = self.cnt[sname]
            if prev > 0 and self.wm[eng].get(sname, 0) < prev:
                waits.append((sname, prev))
                self.wm[eng][sname] = prev
        elif inc:
            sname, amt = self.CSEM[eng], 1
        else:
            sname, amt = None, 0
        tok = None
        if sname is not None:
            self.cnt[sname] += amt
            tok = (sname, self.cnt[sname], None if dma else eng)
        self.q[eng].append((waits, fn, sname, amt))
        return tok

    def dma(self, eng, out, in_, deps=(), **kw):
        return self.emit(eng, lambda e: e.dma_start(out, in_, **kw), deps, dma=True)

    def dmat(self, eng, out, in_, deps=()):
        return self.emit(eng, lambda e: e.dma_start_transpose(out, in_), deps, dma=True)

    def fence(self):
        return [(s, self.cnt[s], None) for s in self.cnt if self.cnt[s] > 0]

    def finalize(self, block):
        final = {s: self.cnt[s] for s in self.cnt}

        def mk(name):
            def body(eng):
                for (waits, fn, sname, amt) in self.q[name]:
                    for (s, v) in waits:
                        eng.wait_ge(self.semh[s], v)
                    inst = fn(eng)
                    if sname is not None:
                        inst.then_inc(self.semh[sname], amt)
                if name == "gpsimd":
                    for s, v in final.items():
                        if v > 0:
                            eng.wait_ge(self.semh[s], v)
            return body
        for n in self.ENG:
            getattr(block, n)(mk(n))


def build_A(n_sub=12, do_l0=True, do_combine=True, dump=None):
    nc = bass.Bass("TRN2", target_bir_lowering=False)
    stack = ExitStack()
    dt = nc.dram_tensor

    x_slab = dt("x_slab", [S, D], F32, kind="ExternalInput").ap()
    l0_wqkv = dt("l0_wqkv", [D, 3 * D], F32, kind="ExternalInput").ap()
    l0_wo = dt("l0_wo", [D, D], F32, kind="ExternalInput").ap()
    l0_bqkv = dt("l0_bqkv", [1, 3 * D], F32, kind="ExternalInput").ap()
    l0_bo = dt("l0_bo", [1, D], F32, kind="ExternalInput").ap()
    wqkv6 = dt("wqkv6", [6, D, 3 * D], F32, kind="ExternalInput").ap()
    wo6 = dt("wo6", [6, D, D], F32, kind="ExternalInput").ap()
    w1p6 = dt("w1p6", [6, D, 2 * DI2], F32, kind="ExternalInput").ap()
    b1p6 = dt("b1p6", [6, 1, 2 * DI2], F32, kind="ExternalInput").ap()
    w2p6 = dt("w2p6", [6, DI2, D], F32, kind="ExternalInput").ap()
    b26 = dt("b26", [6, 1, D], F32, kind="ExternalInput").ap()
    lng6 = dt("lng6", [6, 128, 8], F32, kind="ExternalInput").ap()
    lnb6 = dt("lnb6", [6, 128, 8], F32, kind="ExternalInput").ap()
    qs6 = dt("qs6", [6, DH], F32, kind="ExternalInput").ap()
    ks6 = dt("ks6", [6, DH], F32, kind="ExternalInput").ap()
    rotp = dt("rotp", [S, 4, DH], F32, kind="ExternalInput").ap()
    masks_in = dt("masks_in", [3, 128, 384], F32, kind="ExternalInput").ap()
    am_in = dt("am_in", [2, 128, 384], F32, kind="ExternalInput").ap()
    tokm_in = dt("tokm_in", [128, NCH], F32, kind="ExternalInput").ap()
    eye_in = dt("eye_in", [128, 128], F32, kind="ExternalInput").ap()

    x1_out = dt("x1", [REALT, D], F32, kind="ExternalOutput").ap()
    dump_y = dump_a0 = None
    dump_multi = {}
    if dump == "y":
        dump_y = dt("dbg_y", [S, D], F32, kind="ExternalOutput").ap()
    if dump == "a0":
        dump_a0 = dt("dbg_a0", [REALT, D], F32, kind="ExternalOutput").ap()
    if dump == "ymulti":
        for s_ in range(2, n_sub + 1, 2):
            dump_multi[s_] = dt(f"dbg_y{s_}", [REALT, D], F32, kind="ExternalOutput").ap()

    y32 = [dt("y32a", [S + 1, D], F32).ap(), dt("y32b", [S + 1, D], F32).ap()]
    ybf = [dt("ybfa", [S + 1, D], BF16).ap(), dt("ybfb", [S + 1, D], BF16).ap()]
    hbuf = dt("hbuf", [DI2, S], BF16).ap()
    a0buf = dt("a0buf", [REALT, D], F32).ap()
    b1escr = dt("b1escr", [1, 2 * DI2], F32).ap()

    st = nc.sbuf_tensor
    with stack:
        arena = stack.enter_context(st("arena", [128, 45056], BF16))
        zf = stack.enter_context(st("zf", [128, 2, 8, 256], BF16))
        zst = stack.enter_context(st("zst", [128, 1024], BF16))
        rt = stack.enter_context(st("rt", [128, 2, 4, DH], F32))
        qkb = stack.enter_context(st("qkb", [128, 2, 2048], BF16))
        sqa = stack.enter_context(st("sqa", [128, 1024], BF16))
        qbt = stack.enter_context(st("qbt", [128, 1024], BF16))
        qrot = stack.enter_context(st("qrot", [128, 2048], BF16))
        rn16 = stack.enter_context(st("rn16", [128, 16], F32))
        rnsq = stack.enter_context(st("rnsq", [128, 16], F32))
        rnr = stack.enter_context(st("rnr", [128, 16], F32))
        e1 = stack.enter_context(st("e1", [128, DH], F32))
        e2 = stack.enter_context(st("e2", [128, DH], F32))
        f1t = stack.enter_context(st("f1t", [128, 16, DH], BF16))
        f2t = stack.enter_context(st("f2t", [128, 16, DH], BF16))
        vring = stack.enter_context(st("vring", [128, 2, 2, 1024], BF16))
        kfm = stack.enter_context(st("kfm", [128, 2, 8, 256], BF16))
        qfm = stack.enter_context(st("qfm", [128, 8, 256], BF16))
        s32 = stack.enter_context(st("s32", [128, 2, 384], F32))
        a32 = stack.enter_context(st("a32", [128, 384], F32))
        abf = stack.enter_context(st("abf", [128, 384], BF16))
        aTs = stack.enter_context(st("aTs", [128, 2, 384], BF16))
        ctxs = stack.enter_context(st("ctxs", [128, 8, 256], BF16))
        msk = stack.enter_context(st("msk", [128, 5, 384], BF16))
        y32t = stack.enter_context(st("y32t", [128, 2, 1024], F32))
        yout = stack.enter_context(st("yout", [128, 2, 1024], F32))
        tokm = stack.enter_context(st("tokm", [128, NCH], F32))
        eye = stack.enter_context(st("eye", [128, 128], BF16))
        onesr = stack.enter_context(st("onesr", [1, 128], BF16))
        qkvbr = stack.enter_context(st("qkvbr", [1, 3 * D], BF16))
        obr = stack.enter_context(st("obr", [1, D], BF16))
        b2r = stack.enter_context(st("b2r", [1, D], BF16))
        b1row = stack.enter_context(st("b1row", [1, 2 * DI2], BF16))
        qsr = stack.enter_context(st("qsr", [128, DH], F32))
        qsw = stack.enter_context(st("qsw", [128, DH], F32))
        ksr = stack.enter_context(st("ksr", [128, DH], F32))
        ksw = stack.enter_context(st("ksw", [128, DH], F32))
        gcol = stack.enter_context(st("gcol", [128, 8], F32))
        bcol = stack.enter_context(st("bcol", [128, 8], BF16))
        b1e = stack.enter_context(st("b1e", [128, 2 * NJP], F32))
        hfmt = stack.enter_context(st("hfmt", [128, 256], BF16))
        glb = stack.enter_context(st("glb", [128, 256], BF16))
        ab2 = stack.enter_context(st("ab2", [128, 256], BF16))
        hall = stack.enter_context(st("hall", [128, NJP, 256], BF16))
        colA = stack.enter_context(st("colA", [128, 1], F32))
        colB = stack.enter_context(st("colB", [128, 1], F32))
        colC = stack.enter_context(st("colC", [128, 1], F32))
        colD = stack.enter_context(st("colD", [128, 1], F32))
        psA = stack.enter_context(nc.psum_tensor("psA", [128, 2048], F32))
        psB = stack.enter_context(nc.psum_tensor("psB", [128, 1024], F32))
        psC = stack.enter_context(nc.psum_tensor("psC", [128, 512], F32))
        psT = stack.enter_context(nc.psum_tensor("psT", [128, 1024], BF16))

        p = Prog(nc, stack)

        A_qkv = arena[:, 0:24576].rearrange("p (k n) -> p k n", k=8)
        A_wo = arena[:, 24576:32768].rearrange("p (k n) -> p k n", k=8)
        A_w1 = arena[:, 0:45056].rearrange("p (k n) -> p k n", k=8)
        A_w2 = arena[:, 0:22528].rearrange("p (k n) -> p k n", k=NJP)

        arena_b = Buf()
        zf_b = [Buf(), Buf()]
        zst_b = [Buf()]
        rt_b = Buf()
        qkb_b = [Buf(), Buf()]
        sqa_b = Buf()
        qbt_b = Buf()
        qrot_b = Buf()
        rn_b = Buf()
        ee_b = Buf()
        ff_b = Buf()
        vr_b = [Buf(), Buf()]
        kf_b = [Buf(), Buf()]
        qf_b = Buf()
        s32_b = [Buf(), Buf()]
        a32_b = Buf()
        abf_b = Buf()
        aT_b = [Buf(), Buf()]
        ctx_b = Buf()
        y32t_b = [Buf(), Buf()]
        yout_b = [Buf(), Buf()]
        bA = [Buf(), Buf(), Buf(), Buf()]
        bB = [Buf(), Buf()]
        bC = Buf()
        bT = Buf()
        col_b = [Buf(), Buf(), Buf(), Buf()]
        hall_b = Buf()
        hfmt_b = Buf()
        glb_b = Buf()
        ab2_b = Buf()
        b1e_b = Buf()
        misc_b = Buf()

        psAv = psA.rearrange("p (b n) -> p b n", b=4)

        p.dma("gpsimd", eye[:], eye_in)
        p.dma("gpsimd", msk[:, 0:3, :], masks_in.rearrange("m q n -> q m n"))
        p.dma("gpsimd", msk[:, 3:5, :], am_in.rearrange("m q n -> q m n"))
        p.dma("sync", tokm[:], tokm_in)
        t_ones = p.emit("gpsimd", lambda e: e.memset(onesr[:], 1.0))
        p.emit("gpsimd", lambda e: e.memset(vring[:].rearrange("p a b n -> p (a b n)"), 0.0))
        p.emit("gpsimd", lambda e: e.memset(kfm[:].rearrange("p a b n -> p (a b n)"), 0.0))
        p.emit("vector", lambda e: e.memset(yout[:, 0, :], 0.0))
        tz = p.emit("vector", lambda e: e.memset(yout[:, 1, :], 0.0))
        for b in range(2):
            t1 = p.dma("sync", y32[b][0:1, :], yout[0:1, 0, :], deps=[tz])
            t2 = p.dma("gpsimd", ybf[b][0:1, :], yout[0:1, 0, :], deps=[tz])
            yout_b[0].did_read(t1)
            yout_b[0].did_read(t2)
        yout_b[0].did_write(tz)
        yout_b[1].did_write(tz)

        def emit_ln(src_tile_ap, src_toks, out_ap, out_buf):
            t1 = p.emit("vector", lambda e: e.reduce_sum(colA[:], src_tile_ap, axis=AX.X),
                        deps=src_toks + col_b[0].wdep())
            col_b[0].did_write(t1)
            t2 = p.emit("vector", lambda e: e.tensor_scalar_mul(colB[:], colA[:], -1.0 / D),
                        deps=[t1] + col_b[1].wdep())
            col_b[1].did_write(t2)
            col_b[0].did_read(t2)
            t3 = p.emit("scalar", lambda e: e.activation(
                sqa[:], src_tile_ap, AF.Square, bias=colB[:], scale=1.0, accum_out=colC[:]),
                deps=src_toks + [t2] + sqa_b.wdep() + col_b[2].wdep())
            sqa_b.did_write(t3)
            col_b[2].did_write(t3)
            col_b[1].did_read(t3)
            t4 = p.emit("vector", lambda e: e.tensor_scalar(colC[:], colC[:], 1.0 / D, 1e-5, ALU.mult, ALU.add),
                        deps=[t3])
            col_b[2].did_write(t4)
            t5 = p.emit("scalar", lambda e: e.activation(colC[:], colC[:], AF.Sqrt), deps=[t4])
            col_b[2].did_write(t5)
            t6 = p.emit("vector", lambda e: e.reciprocal(colD[:], colC[:]), deps=[t5] + col_b[3].wdep())
            col_b[3].did_write(t6)
            col_b[2].did_read(t6)
            t7 = p.emit("vector", lambda e: e.tensor_scalar(
                out_ap, src_tile_ap, colB[:], colD[:], ALU.add, ALU.mult),
                deps=src_toks + [t2, t6] + out_buf.wdep())
            out_buf.did_write(t7)
            col_b[1].did_read(t7)
            col_b[3].did_read(t7)
            sqa_b.did_read(t4)
            return t7

        for c in range(NCH):
            sl = c % 2
            tl = p.dma("sync", y32t[:, sl, :], x_slab[c * 128:(c + 1) * 128, :],
                       deps=y32t_b[sl].wdep())
            y32t_b[sl].did_write(tl)
            t7 = emit_ln(y32t[:, sl, :], [tl], yout[:, sl, :], yout_b[sl])
            y32t_b[sl].did_read(t7)
            ts1 = p.dma("sync", y32[0][c * 128 + 1:(c + 1) * 128 + 1, :], yout[:, sl, :], deps=[t7])
            ts2 = p.dma("gpsimd", ybf[0][c * 128 + 1:(c + 1) * 128 + 1, :], yout[:, sl, :], deps=[t7])
            yout_b[sl].did_read(ts1)
            yout_b[sl].did_read(ts2)

        def pe_tr(srcs, dst_view, deps, dst_buf):
            tts = None
            n = len(srcs)
            for i, sv in enumerate(srcs):
                tts = p.emit("tensor", lambda e, sv=sv, i=i: e.transpose(
                    psT[:, i * 128:(i + 1) * 128], sv, eye[:]),
                    deps=(list(deps) + bT.wdep()) if i == 0 else [], inc=(i == n - 1))
            bT.did_write(tts)
            tev = p.emit("scalar", lambda e, n=n: e.copy(
                dst_view, psT[:, 0:n * 128].rearrange("p (k m) -> p k m", k=n)),
                deps=[tts] + dst_buf.wdep())
            dst_buf.did_write(tev)
            bT.did_read(tev)
            return tts, tev

        def emit_attn(src, srcbf, dstpair, cfg):
            ent = p.fence()
            wl = []
            for hku in range(2):
                w = p.dma("gpsimd", A_qkv[:, 4 * hku:4 * hku + 4, :],
                          cfg["wqkv_dram"].rearrange("(k p) n -> p k n", p=128)[:, 4 * hku:4 * hku + 4, :],
                          deps=ent + arena_b.wdep())
                wl.append(w)
            w = p.dma("gpsimd", A_wo[:, :, :], cfg["wo_dram"].rearrange("(k p) n -> p k n", p=128),
                      deps=ent + arena_b.wdep())
            wl.append(w)
            bias_toks = [t_ones]
            if cfg["bias"]:
                bias_toks.append(p.dma("gpsimd", qkvbr[:], l0_bqkv, deps=ent))
                bias_toks.append(p.dma("gpsimd", obr[:], l0_bo, deps=ent))
            for w in wl:
                arena_b.did_write(w)
            L = cfg.get("L")
            qs_tok = []
            if cfg["rot"]:
                r1 = p.dma("sync", qsr[:], qs6[L:L + 1, :].broadcast_to([128, DH]),
                           deps=ent + misc_b.wdep())
                r2 = p.dma("sync", ksr[:], ks6[L:L + 1, :].broadcast_to([128, DH]),
                           deps=ent + misc_b.wdep())
                c1 = p.emit("gpsimd", lambda e: e.tensor_copy(qsw[:, 0:32], qsr[:, 32:64]), deps=[r1])
                c2 = p.emit("gpsimd", lambda e: e.tensor_copy(qsw[:, 32:64], qsr[:, 0:32]), deps=[r1])
                c3 = p.emit("gpsimd", lambda e: e.tensor_copy(ksw[:, 0:32], ksr[:, 32:64]), deps=[r2])
                c4 = p.emit("gpsimd", lambda e: e.tensor_copy(ksw[:, 32:64], ksr[:, 0:32]), deps=[r2])
                misc_b.did_write(c4)
                qs_tok = [r1, r2, c1, c2, c3, c4]

            scale = cfg["scale"]
            out_a0 = cfg["out"] == "a0"

            for j in range(cfg["jlo"], NAT):
                sl = j % 2
                psl = (j - 1) % 2
                do_q = j >= cfg["q_from"]
                r0 = 256 * j
                zt = []
                for ts in range(2):
                    rq = r0 + ts * 128
                    tz1 = p.dma("sync", zst[:, 0:512], srcbf[rq + 1:rq + 129, 0:512],
                                deps=zst_b[0].wdep() + ent)
                    ro = rq if cfg["shift"] else rq + 1
                    tz2 = p.dma("sync", zst[:, 512:1024], srcbf[ro:ro + 128, 512:1024],
                                deps=zst_b[0].wdep() + ent)
                    zst_b[0].did_write(tz2)
                    zst_b[0].did_write(tz1)
                    srcs = [zst[:, k * 128:(k + 1) * 128] for k in range(8)]
                    tts, tev = pe_tr(srcs, zf[:, sl, :, ts * 128:ts * 128 + 128],
                                     [tz1, tz2] + ent + (zf_b[sl].wdep() if ts == 0 else []),
                                     zf_b[sl])
                    zst_b[0].did_read(tts)
                    zt.append(tev)
                tr = None
                if cfg["rot"]:
                    tr = p.dma("sync", rt[:],
                               rotp[r0:r0 + 256, :, :].rearrange("(s p) f d -> p s f d", p=128),
                               deps=rt_b.wdep() + ent)
                    rt_b.did_write(tr)

                for ts in range(2):
                    for nt in range(6):
                        bank = nt % 2
                        bdeps = bA[bank].wdep() + zt + wl + bias_toks + ent
                        for k in range(8):
                            last = (not cfg["bias"]) and k == 7
                            tm = p.emit("tensor", lambda e, ts=ts, nt=nt, k=k, bank=bank, sl=sl: e.matmul(
                                psAv[:, bank, :], zf[:, sl, k, ts * 128:ts * 128 + 128],
                                A_qkv[:, k, nt * 512:nt * 512 + 512],
                                start=(k == 0), stop=last),
                                deps=(bdeps if k == 0 else []), inc=last)
                        if cfg["bias"]:
                            tm = p.emit("tensor", lambda e, nt=nt, bank=bank: e.matmul(
                                psAv[:, bank, :], onesr[:], qkvbr[:, nt * 512:nt * 512 + 512],
                                start=False, stop=True), inc=True)
                        bA[bank].did_write(tm)
                        zf_b[sl].did_read(tm)
                        if nt < 4:
                            dst = qkb[:, ts, nt * 512:nt * 512 + 512]
                            te = p.emit("vector", lambda e, dst=dst, bank=bank: e.tensor_copy(dst, psAv[:, bank, :]),
                                        deps=[tm] + (qkb_b[ts].wdep() if nt == 0 else []))
                            qkb_b[ts].did_write(te)
                        else:
                            dst = vring[:, sl, ts, (nt - 4) * 512:(nt - 4) * 512 + 512]
                            te = p.emit("vector", lambda e, dst=dst, bank=bank: e.tensor_copy(dst, psAv[:, bank, :]),
                                        deps=[tm] + (vr_b[sl].wdep() if (nt == 4 and ts == 0) else []))
                            vr_b[sl].did_write(te)
                        bA[bank].did_read(te)

                for ts in range(2):
                    qk_tok = qkb_b[ts].w
                    if cfg["rot"]:
                        for (base, rep, swp, t_c, t_s) in ((0, qsr, qsw, 0, 1), (1024, ksr, ksw, 2, 3)):
                            qv = qkb[:, ts, base:base + 1024].rearrange("p (h d) -> p h d", h=16)
                            t1 = p.emit("scalar", lambda e, qv=qv: e.activation(sqa[:], qv, AF.Square),
                                        deps=[qk_tok] + sqa_b.wdep())
                            sqa_b.did_write(t1)
                            t2 = p.emit("vector", lambda e: e.reduce_sum(
                                rn16[:], sqa[:].rearrange("p (h d) -> p h d", h=16), axis=AX.X),
                                deps=[t1] + rn_b.wdep())
                            sqa_b.did_read(t2)
                            t3 = p.emit("scalar", lambda e: e.activation(rnsq[:], rn16[:], AF.Sqrt), deps=[t2])
                            t4 = p.emit("vector", lambda e: e.tensor_scalar_max(rnsq[:], rnsq[:], 1e-12), deps=[t3])
                            t5 = p.emit("vector", lambda e: e.reciprocal(rnr[:], rnsq[:]), deps=[t4])
                            rn_b.did_write(t5)
                            te1 = p.emit("gpsimd", lambda e, rep=rep, t_c=t_c, ts=ts: e.tensor_tensor(
                                e1[:], rep[:], rt[:, ts, t_c, :], ALU.mult),
                                deps=[tr] + qs_tok + ee_b.wdep())
                            te2 = p.emit("gpsimd", lambda e, swp=swp, t_s=t_s, ts=ts: e.tensor_tensor(
                                e2[:], swp[:], rt[:, ts, t_s, :], ALU.mult),
                                deps=[tr] + qs_tok + ee_b.wdep())
                            ee_b.did_write(te2)
                            rt_b.did_read(te1)
                            rt_b.did_read(te2)
                            tf1 = p.emit("vector", lambda e: e.tensor_tensor(
                                f1t[:], rnr[:].unsqueeze(2).broadcast_to([128, 16, DH]),
                                e1[:].unsqueeze(1).broadcast_to([128, 16, DH]), ALU.mult),
                                deps=[t5, te1, te2] + ff_b.wdep())
                            tf2 = p.emit("gpsimd", lambda e: e.tensor_tensor(
                                f2t[:], rnr[:].unsqueeze(2).broadcast_to([128, 16, DH]),
                                e2[:].unsqueeze(1).broadcast_to([128, 16, DH]), ALU.mult),
                                deps=[t5, te2] + ff_b.wdep())
                            ff_b.did_write(tf2)
                            rn_b.did_read(tf1)
                            rn_b.did_read(tf2)
                            ee_b.did_read(tf1)
                            ee_b.did_read(tf2)
                            qa = sqa[:].rearrange("p (h d) -> p h d", h=16)
                            qbv = qbt[:].rearrange("p (h d) -> p h d", h=16)
                            ta = p.emit("vector", lambda e, qv=qv, qa=qa: e.tensor_tensor(
                                qa, qv, f1t[:], ALU.mult),
                                deps=[tf1, qk_tok] + sqa_b.wdep())
                            sqa_b.did_write(ta)
                            tb1 = p.emit("gpsimd", lambda e, qv=qv, qbv=qbv: e.tensor_tensor(
                                qbv[:, :, 0:32], qv[:, :, 32:64], f2t[:, :, 0:32], ALU.mult),
                                deps=[tf2, qk_tok] + qbt_b.wdep())
                            tb2 = p.emit("gpsimd", lambda e, qv=qv, qbv=qbv: e.tensor_tensor(
                                qbv[:, :, 32:64], qv[:, :, 0:32], f2t[:, :, 32:64], ALU.mult),
                                deps=[tf2, qk_tok] + qbt_b.wdep())
                            qbt_b.did_write(tb2)
                            ff_b.did_read(ta)
                            ff_b.did_read(tb1)
                            ff_b.did_read(tb2)
                            tq = p.emit("vector", lambda e, base=base: e.tensor_add(
                                qrot[:, base:base + 1024], sqa[:], qbt[:]),
                                deps=[ta, tb1, tb2] + qrot_b.wdep())
                            qrot_b.did_write(tq)
                            sqa_b.did_read(tq)
                            qbt_b.did_read(tq)
                        qsrc = qrot
                        qsrc_b = qrot_b
                        qsrc_tok = qrot_b.w
                    else:
                        qsrc = qkb[:, ts, :]
                        qsrc_b = qkb_b[ts]
                        qsrc_tok = qk_tok
                    srcsq = [qsrc[:, hp * 128:(hp + 1) * 128] for hp in range(8)]
                    ttsq, tevq = pe_tr(srcsq, qfm[:, :, ts * 128:ts * 128 + 128],
                                       [qsrc_tok] + ent + (qf_b.wdep() if ts == 0 else []), qf_b)
                    qsrc_b.did_read(ttsq)
                    srcsk = [qsrc[:, 1024 + hp * 128:1024 + (hp + 1) * 128] for hp in range(8)]
                    ttsk, tevk = pe_tr(srcsk, kfm[:, sl, :, ts * 128:ts * 128 + 128],
                                       [qsrc_tok] + ent + (kf_b[sl].wdep() if ts == 0 else []), kf_b[sl])
                    qsrc_b.did_read(ttsk)

                if not do_q:
                    continue

                fm_toks = [qf_b.w, kf_b[sl].w, kf_b[psl].w]

                for ql in range(2):
                    qc = 2 * j + ql
                    if out_a0 and qc < 14:
                        continue
                    if cfg["maskmode"] == "inner":
                        mslot = {0: 1, 1: 2, 14: 3, 15: 4}.get(qc, 0)
                    else:
                        mslot = 3 if qc == 14 else (4 if qc == 15 else 0)
                    if ql == 0:
                        runs = [(psl, 0, 256), (sl, 0, 128)]
                    else:
                        runs = [(psl, 128, 128), (sl, 0, 256)]
                    for h in range(16):
                        hp, half = h // 2, h % 2
                        sb = h % 2
                        lhs = qfm[64 * half:64 * half + 64, hp, ql * 128:ql * 128 + 128]
                        coff = 0
                        smm = []
                        bdeps = bB[sb].wdep() + fm_toks + ent
                        for (rsl, rcol, rn_) in runs:
                            tm = p.emit("tensor",
                                        lambda e, rsl=rsl, rcol=rcol, rn_=rn_, coff=coff, lhs=lhs, sb=sb,
                                        half=half, hp=hp: e.matmul(
                                            psB[:, sb * 512 + coff:sb * 512 + coff + rn_],
                                            lhs, kfm[64 * half:64 * half + 64, rsl, hp, rcol:rcol + rn_],
                                            start=True, stop=True),
                                        deps=bdeps, inc=True)
                            smm.append(tm)
                            kf_b[rsl].did_read(tm)
                            qf_b.did_read(tm)
                            coff += rn_
                        bB[sb].did_write(smm[-1])
                        tev = p.emit("vector", lambda e, sb=sb, mslot=mslot, ql=ql: e.scalar_tensor_tensor(
                            s32[:, ql, :], psB[:, sb * 512:sb * 512 + 384], scale,
                            msk[:, mslot, :], ALU.mult, ALU.add),
                            deps=smm + s32_b[ql].wdep() + ent)
                        s32_b[ql].did_write(tev)
                        bB[sb].did_read(tev)
                        tmx = p.emit("vector", lambda e, ql=ql: e.reduce_max(
                            colA[:], s32[:, ql, :], axis=AX.X, negate=True),
                            deps=[tev] + col_b[0].wdep())
                        col_b[0].did_write(tmx)
                        tex = p.emit("scalar", lambda e, ql=ql: e.activation(
                            a32[:], s32[:, ql, :], AF.Exp, bias=colA[:], scale=1.0, accum_out=colB[:]),
                            deps=[tev, tmx] + a32_b.wdep() + col_b[1].wdep())
                        a32_b.did_write(tex)
                        col_b[1].did_write(tex)
                        col_b[0].did_read(tex)
                        s32_b[ql].did_read(tex)
                        trc = p.emit("vector", lambda e: e.reciprocal(colC[:], colB[:]),
                                     deps=[tex] + col_b[2].wdep())
                        col_b[2].did_write(trc)
                        col_b[1].did_read(trc)
                        tnm = p.emit("vector", lambda e: e.tensor_scalar_mul(abf[:], a32[:], colC[:]),
                                     deps=[trc] + abf_b.wdep())
                        abf_b.did_write(tnm)
                        a32_b.did_read(tnm)
                        col_b[2].did_read(tnm)
                        atl = h % 2
                        tts = None
                        for c in range(3):
                            tts = p.emit("tensor", lambda e, c=c: e.transpose(
                                psT[:, c * 128:c * 128 + 128], abf[:, c * 128:c * 128 + 128], eye[:]),
                                deps=([tnm] + bT.wdep() + ent) if c == 0 else [], inc=(c == 2))
                        bT.did_write(tts)
                        abf_b.did_read(tts)
                        tta = p.emit("scalar", lambda e, atl=atl: e.copy(aTs[:, atl, :], psT[:, 0:384]),
                                     deps=[tts] + aT_b[atl].wdep())
                        aT_b[atl].did_write(tta)
                        bT.did_read(tta)
                        avv = None
                        for c in range(3):
                            kc = qc - 2 + c
                            asl, ats = (kc // 2) % 2, kc % 2
                            avv = p.emit("tensor",
                                         lambda e, c=c, asl=asl, ats=ats, half=half, ql=ql, atl=atl, h=h: e.matmul(
                                             psC[64 * half:64 * half + 64, ql * 128:ql * 128 + 128],
                                             vring[:, asl, ats, h * 64:h * 64 + 64],
                                             aTs[:, atl, c * 128:c * 128 + 128],
                                             start=(c == 0), stop=(c == 2)),
                                         deps=(([tta, vr_b[0].w, vr_b[1].w] + bC.wdep())
                                               if c == 0 else []), inc=(c == 2))
                            vr_b[asl].did_read(avv)
                        bC.did_write(avv)
                        aT_b[atl].did_read(avv)
                        if half == 1:
                            tcx = p.emit("scalar", lambda e, hp=hp, ql=ql: e.copy(
                                ctxs[:, hp, ql * 128:ql * 128 + 128], psC[:, ql * 128:ql * 128 + 128]),
                                deps=[avv] + (ctx_b.wdep() if (hp == 0) else []))
                            ctx_b.did_write(tcx)
                            bC.did_read(tcx)

                    chunk = qc
                    ytl = ql
                    tly = None
                    if not out_a0:
                        tly = p.dma("sync", y32t[:, ytl, :], src[chunk * 128 + 1:chunk * 128 + 129, :],
                                    deps=y32t_b[ytl].wdep() + ent)
                        y32t_b[ytl].did_write(tly)
                    for nt in range(2):
                        bank = 2 + nt
                        bdeps = bA[bank].wdep() + [ctx_b.w] + bias_toks
                        for k in range(8):
                            last = (not cfg["bias"]) and k == 7
                            tm = p.emit("tensor", lambda e, nt=nt, k=k, ql=ql, bank=bank: e.matmul(
                                psAv[:, bank, :], ctxs[:, k, ql * 128:ql * 128 + 128],
                                A_wo[:, k, nt * 512:nt * 512 + 512],
                                start=(k == 0), stop=last),
                                deps=(bdeps if k == 0 else []), inc=last)
                        if cfg["bias"]:
                            tm = p.emit("tensor", lambda e, nt=nt, bank=bank: e.matmul(
                                psAv[:, bank, :], onesr[:], obr[:, nt * 512:nt * 512 + 512],
                                start=False, stop=True), inc=True)
                        bA[bank].did_write(tm)
                        ctx_b.did_read(tm)
                        if out_a0:
                            tev = p.emit("vector", lambda e, nt=nt, ytl=ytl, bank=bank: e.tensor_copy(
                                yout[:, ytl, nt * 512:nt * 512 + 512], psAv[:, bank, :]),
                                deps=[tm] + (yout_b[ytl].wdep() if nt == 0 else []))
                        else:
                            tev = p.emit("vector",
                                         lambda e, nt=nt, ytl=ytl, chunk=chunk, bank=bank: e.scalar_tensor_tensor(
                                             yout[:, ytl, nt * 512:nt * 512 + 512], psAv[:, bank, :],
                                             tokm[:, chunk:chunk + 1], y32t[:, ytl, nt * 512:nt * 512 + 512],
                                             ALU.mult, ALU.add),
                                         deps=[tm, tly] + ent + (yout_b[ytl].wdep() if nt == 0 else []))
                            y32t_b[ytl].did_read(tev)
                        yout_b[ytl].did_write(tev)
                        bA[bank].did_read(tev)
                    if out_a0:
                        r_out = chunk * 128 - PADT
                        ts1 = p.dma("sync", a0buf[r_out:r_out + 128, :], yout[:, ytl, :], deps=[yout_b[ytl].w])
                        yout_b[ytl].did_read(ts1)
                    else:
                        ts1 = p.dma("sync", dstpair[0][chunk * 128 + 1:chunk * 128 + 129, :], yout[:, ytl, :],
                                    deps=[yout_b[ytl].w])
                        ts2 = p.dma("gpsimd", dstpair[1][chunk * 128 + 1:chunk * 128 + 129, :], yout[:, ytl, :],
                                    deps=[yout_b[ytl].w])
                        yout_b[ytl].did_read(ts1)
                        yout_b[ytl].did_read(ts2)

        def emit_ffn(src, dstpair, L):
            ent = p.fence()
            wl = []
            for hku in range(2):
                w = p.dma("gpsimd", A_w1[:, 4 * hku:4 * hku + 4, :],
                          w1p6[L].rearrange("(k p) n -> p k n", p=128)[:, 4 * hku:4 * hku + 4, :],
                          deps=ent + arena_b.wdep())
                wl.append(w)
            tg = p.dma("sync", gcol[:], lng6[L], deps=ent + misc_b.wdep())
            tb = p.dma("gpsimd", bcol[:], lnb6[L], deps=ent)
            tb1 = p.dma("gpsimd", b1row[:], b1p6[L], deps=ent)
            misc_b.did_write(tb)
            sc = []
            for k in range(8):
                tsc = p.emit("gpsimd", lambda e, k=k: e.tensor_scalar_mul(
                    A_w1[:, k, :], A_w1[:, k, :], gcol[:, k:k + 1]),
                    deps=wl + [tg])
                sc.append(tsc)
            for t_ in sc:
                arena_b.did_write(t_)
            bst = []
            for nt in range(11):
                bk = nt % 2
                tm = None
                for k in range(8):
                    tm = p.emit("tensor", lambda e, nt=nt, k=k, bk=bk: e.matmul(
                        psB[0:1, bk * 512:bk * 512 + 512], bcol[:, k:k + 1],
                        A_w1[:, k, nt * 512:nt * 512 + 512], start=(k == 0), stop=False),
                        deps=((sc + [tb, t_ones] + bB[bk].wdep()) if k == 0 else []), inc=False)
                tm = p.emit("tensor", lambda e, nt=nt, bk=bk: e.matmul(
                    psB[0:1, bk * 512:bk * 512 + 512], onesr[:, 0:1], b1row[:, nt * 512:nt * 512 + 512],
                    start=False, stop=True), deps=[tb1], inc=True)
                bB[bk].did_write(tm)
                tev = p.emit("vector", lambda e, bk=bk: e.tensor_copy(
                    yout[0:1, 0, 0:512], psB[0:1, bk * 512:bk * 512 + 512]),
                    deps=[tm] + yout_b[0].wdep())
                yout_b[0].did_write(tev)
                bB[bk].did_read(tev)
                tss = p.dma("sync", b1escr[:, nt * 512:nt * 512 + 512], yout[0:1, 0, 0:512], deps=[tev])
                yout_b[0].did_read(tss)
                bst.append(tss)
            b1ecol = b1escr.rearrange("o (c p) -> (o c) p", p=128)
            tld = None
            for cc in range(2 * NJP):
                tld = p.dma("sync", b1e[:, cc:cc + 1],
                            b1ecol[cc:cc + 1, :].transpose([1, 0]),
                            deps=bst + (b1e_b.wdep() if cc == 0 else []))
            b1e_b.did_write(tld)
            b1dep = [tld]

            for tile in range(NAT):
                r0 = 256 * tile
                for c2 in range(2):
                    chunk = 2 * tile + c2
                    tl1 = p.dma("sync", y32t[:, c2, 0:512], src[chunk * 128 + 1:chunk * 128 + 129, 0:512],
                                deps=y32t_b[c2].wdep() + ent)
                    tl2 = p.dma("sync", y32t[:, c2, 512:1024], src[chunk * 128:chunk * 128 + 128, 512:1024],
                                deps=ent)
                    y32t_b[c2].did_write(tl2)
                    y32t_b[c2].did_write(tl1)
                    t7 = emit_ln(y32t[:, c2, :], [tl1, tl2], qbt[:], qbt_b)
                    y32t_b[c2].did_read(t7)
                    srcs = [qbt[:, k * 128:(k + 1) * 128] for k in range(8)]
                    tts, tev = pe_tr(srcs, zf[:, 0, :, c2 * 128:c2 * 128 + 128],
                                     [t7] + ent + (zf_b[0].wdep() if c2 == 0 else []), zf_b[0])
                    qbt_b.did_read(tts)
                zfm_tok = [zf_b[0].w]
                for jp in range(NJP):
                    ba = 2 * (jp % 2)
                    bg = ba + 1
                    amm = gmm = None
                    for (bank, joff) in ((ba, jp), (bg, NJP + jp)):
                        bdeps = bA[bank].wdep() + zfm_tok + sc + ent
                        tm = None
                        for k in range(8):
                            tm = p.emit("tensor", lambda e, k=k, joff=joff, bank=bank: e.matmul(
                                psAv[:, bank, 0:256], A_w1[:, k, joff * 128:joff * 128 + 128],
                                zf[:, 0, k, :], start=(k == 0), stop=(k == 7)),
                                deps=(bdeps if k == 0 else []), inc=(k == 7))
                        bA[bank].did_write(tm)
                        zf_b[0].did_read(tm)
                        if bank == ba:
                            amm = tm
                        else:
                            gmm = tm
                    tgl = p.emit("scalar", lambda e, bg=bg, jp=jp: e.activation(
                        glb[:], psAv[:, bg, 0:256], AF.Gelu, bias=b1e[:, NJP + jp:NJP + jp + 1], scale=1.0),
                        deps=[gmm] + b1dep + glb_b.wdep())
                    glb_b.did_write(tgl)
                    bA[bg].did_read(tgl)
                    b1e_b.did_read(tgl)
                    tab = p.emit("vector", lambda e, ba=ba, jp=jp: e.tensor_scalar_add(
                        ab2[:], psAv[:, ba, 0:256], b1e[:, jp:jp + 1]),
                        deps=[amm] + b1dep + ab2_b.wdep())
                    ab2_b.did_write(tab)
                    bA[ba].did_read(tab)
                    b1e_b.did_read(tab)
                    thm = p.emit("gpsimd", lambda e: e.tensor_tensor(hfmt[:], ab2[:], glb[:], ALU.mult),
                                 deps=[tgl, tab] + hfmt_b.wdep())
                    hfmt_b.did_write(thm)
                    glb_b.did_read(thm)
                    ab2_b.did_read(thm)
                    tss = p.dma("sync", hbuf[jp * 128:jp * 128 + 128, r0:r0 + 256], hfmt[:], deps=[thm])
                    hfmt_b.did_read(tss)

            fence1 = p.fence()
            w2l = p.dma("gpsimd", A_w2[:, :, :], w2p6[L].rearrange("(c p) n -> p c n", p=128),
                        deps=fence1 + arena_b.wdep())
            arena_b.did_write(w2l)
            tb2 = p.dma("gpsimd", b2r[:], b26[L], deps=ent)
            for tile in range(NAT):
                r0 = 256 * tile
                tlh = p.dma("sync", hall[:],
                            hbuf.rearrange("(c p) n -> p c n", p=128)[:, :, r0:r0 + 256],
                            deps=fence1 + hall_b.wdep())
                hall_b.did_write(tlh)
                tly = {}
                for c2 in range(2):
                    chunk = 2 * tile + c2
                    tly[c2] = p.dma("sync", y32t[:, c2, :], src[chunk * 128 + 1:chunk * 128 + 129, :],
                                    deps=y32t_b[c2].wdep() + fence1)
                    y32t_b[c2].did_write(tly[c2])
                for ts in range(2):
                    chunk = 2 * tile + ts
                    for nt in range(2):
                        bank = 2 * ts + nt
                        bdeps = bA[bank].wdep() + [tlh, w2l, tb2, t_ones]
                        tm = None
                        for jc in range(NJP):
                            tm = p.emit("tensor", lambda e, jc=jc, ts=ts, nt=nt, bank=bank: e.matmul(
                                psAv[:, bank, :], hall[:, jc, ts * 128:ts * 128 + 128],
                                A_w2[:, jc, nt * 512:nt * 512 + 512],
                                start=(jc == 0), stop=False),
                                deps=(bdeps if jc == 0 else []), inc=False)
                        tm = p.emit("tensor", lambda e, nt=nt, bank=bank: e.matmul(
                            psAv[:, bank, :], onesr[:], b2r[:, nt * 512:nt * 512 + 512],
                            start=False, stop=True), inc=True)
                        bA[bank].did_write(tm)
                        hall_b.did_read(tm)
                        tev = p.emit("vector",
                                     lambda e, ts=ts, nt=nt, chunk=chunk, bank=bank: e.scalar_tensor_tensor(
                                         yout[:, ts, nt * 512:nt * 512 + 512], psAv[:, bank, :],
                                         tokm[:, chunk:chunk + 1], y32t[:, ts, nt * 512:nt * 512 + 512],
                                         ALU.mult, ALU.add),
                                     deps=[tm, tly[ts]] + ent + (yout_b[ts].wdep() if nt == 0 else []))
                        yout_b[ts].did_write(tev)
                        y32t_b[ts].did_read(tev)
                        bA[bank].did_read(tev)
                    ts1 = p.dma("sync", dstpair[0][chunk * 128 + 1:chunk * 128 + 129, :], yout[:, ts, :],
                                deps=[yout_b[ts].w])
                    ts2 = p.dma("gpsimd", dstpair[1][chunk * 128 + 1:chunk * 128 + 129, :], yout[:, ts, :],
                                deps=[yout_b[ts].w])
                    yout_b[ts].did_read(ts1)
                    yout_b[ts].did_read(ts2)

        if do_l0:
            emit_attn(y32[0], ybf[0], None, dict(
                rot=False, scale=0.125, wqkv_dram=l0_wqkv, wo_dram=l0_wo, bias=True,
                L=None, jlo=6, q_from=7, out="a0", shift=False, maskmode="l0"))

        for sub in range(n_sub):
            L = sub // 2
            par = sub % 2
            if sub % 2 == 0:
                emit_attn(y32[par], ybf[par], (y32[1 - par], ybf[1 - par]), dict(
                    rot=True, scale=8.0, wqkv_dram=wqkv6[L], wo_dram=wo6[L], bias=False,
                    L=L, jlo=0, q_from=0, out="y", shift=True, maskmode="inner"))
            else:
                emit_ffn(y32[par], (y32[1 - par], ybf[1 - par]), L)
            if (sub + 1) in dump_multi:
                entd = p.fence()
                dy = dump_multi[sub + 1]
                buf = y32[1 - par]
                for c in range(14, NCH):
                    sl = c % 2
                    tl = p.dma("sync", y32t[:, sl, :], buf[c * 128 + 1:(c + 1) * 128 + 1, :],
                               deps=y32t_b[sl].wdep() + entd)
                    y32t_b[sl].did_write(tl)
                    tso = p.dma("sync", dy[c * 128 - PADT:(c + 1) * 128 - PADT, :], y32t[:, sl, :], deps=[tl])
                    y32t_b[sl].did_read(tso)

        fin = n_sub % 2
        if do_combine:
            entc = p.fence()
            for c in range(14, NCH):
                sl = c % 2
                r_real = c * 128 - PADT
                tlx = p.dma("sync", y32t[:, sl, :], x_slab[c * 128:(c + 1) * 128, :],
                            deps=y32t_b[sl].wdep() + entc)
                y32t_b[sl].did_write(tlx)
                tla = p.dma("sync", yout[:, sl, :], a0buf[r_real:r_real + 128, :],
                            deps=yout_b[sl].wdep() + entc)
                yout_b[sl].did_write(tla)
                qv = qbt[:].bitcast(F32)
                sv = sqa[:].bitcast(F32)
                tly1 = p.dma("sync", qv[:, 0:512], y32[fin][c * 128 + 1:c * 128 + 129, 0:512],
                             deps=qbt_b.wdep() + entc)
                qbt_b.did_write(tly1)
                tly2 = p.dma("sync", sv[:, 0:512], y32[fin][c * 128 + 1:c * 128 + 129, 512:1024],
                             deps=sqa_b.wdep() + entc)
                sqa_b.did_write(tly2)
                t1 = p.emit("vector", lambda e, sl=sl: e.tensor_add(y32t[:, sl, :], y32t[:, sl, :], yout[:, sl, :]),
                            deps=[tlx, tla])
                y32t_b[sl].did_write(t1)
                yout_b[sl].did_read(t1)
                t2 = p.emit("vector", lambda e, sl=sl, qv=qv: e.tensor_add(
                    y32t[:, sl, 0:512], y32t[:, sl, 0:512], qv[:, 0:512]), deps=[t1, tly1])
                t3 = p.emit("vector", lambda e, sl=sl, sv=sv: e.tensor_add(
                    y32t[:, sl, 512:1024], y32t[:, sl, 512:1024], sv[:, 0:512]), deps=[t1, tly2])
                y32t_b[sl].did_write(t3)
                qbt_b.did_read(t2)
                sqa_b.did_read(t3)
                tst_ = p.dma("sync", x1_out[r_real:r_real + 128, :], y32t[:, sl, :], deps=[t2, t3])
                y32t_b[sl].did_read(tst_)

        if dump == "y":
            entd = p.fence()
            for c in range(NCH):
                sl = c % 2
                tl = p.dma("sync", y32t[:, sl, :], y32[fin][c * 128 + 1:(c + 1) * 128 + 1, :],
                           deps=y32t_b[sl].wdep() + entd)
                y32t_b[sl].did_write(tl)
                tst_ = p.dma("sync", dump_y[c * 128:(c + 1) * 128, :], y32t[:, sl, :], deps=[tl])
                y32t_b[sl].did_read(tst_)
        if dump == "a0":
            entd = p.fence()
            for c in range(32):
                sl = c % 2
                tl = p.dma("sync", y32t[:, sl, :], a0buf[c * 128:(c + 1) * 128, :],
                           deps=y32t_b[sl].wdep() + entd)
                y32t_b[sl].did_write(tl)
                tst_ = p.dma("sync", dump_a0[c * 128:(c + 1) * 128, :], y32t[:, sl, :], deps=[tl])
                y32t_b[sl].did_read(tst_)

        with nc.Block() as blk:
            p.finalize(blk)

    return nc


def _f32(a):
    return np.ascontiguousarray(a, dtype=np.float32)


def prep_A_shared(ins):
    d = {}
    d["l0_wqkv"] = _f32(np.concatenate(
        [ins["L0_attn_q_w"].T, ins["L0_attn_k_w"].T, ins["L0_attn_v_w"].T], axis=1))
    d["l0_wo"] = _f32(ins["L0_attn_o_w"].T)
    d["l0_bqkv"] = _f32(np.concatenate(
        [ins["L0_attn_q_b"], ins["L0_attn_k_b"], ins["L0_attn_v_b"]])[None, :])
    d["l0_bo"] = _f32(ins["L0_attn_o_b"][None, :])
    d["wqkv6"] = _f32(ins["L0_ffn_Wqkv"])
    d["wo6"] = _f32(ins["L0_ffn_Wo"])
    DI = 2730
    W1 = ins["L0_ffn_W1"]
    w1p = np.zeros((6, D, 2 * DI2), dtype=np.float32)
    w1p[:, :, 0:DI] = W1[:, :, 0:DI]
    w1p[:, :, DI2:DI2 + DI] = W1[:, :, DI:2 * DI]
    d["w1p6"] = w1p
    b1 = ins["L0_ffn_b1"]
    b1p = np.zeros((6, 1, 2 * DI2), dtype=np.float32)
    b1p[:, 0, 0:DI] = b1[:, 0:DI]
    b1p[:, 0, DI2:DI2 + DI] = b1[:, DI:2 * DI]
    d["b1p6"] = b1p
    w2p = np.zeros((6, DI2, D), dtype=np.float32)
    w2p[:, 0:DI, :] = ins["L0_ffn_W2"]
    d["w2p6"] = w2p
    d["b26"] = _f32(ins["L0_ffn_b2"][:, None, :])
    d["lng6"] = _f32(ins["L0_ffn_ln_g"].reshape(6, 8, 128).transpose(0, 2, 1))
    d["lnb6"] = _f32(ins["L0_ffn_ln_b"].reshape(6, 8, 128).transpose(0, 2, 1))
    d["qs6"] = _f32(ins["L0_ffn_q_scale"])
    d["ks6"] = _f32(ins["L0_ffn_k_scale"])
    d["eye_in"] = np.eye(128, dtype=np.float32)
    i = np.arange(128)[:, None]
    col = np.arange(384)[None, :]
    M = np.where((col >= i) & (col <= i + 256), 0.0, NEG).astype(np.float32)
    M1 = np.where(col >= 256, M, NEG).astype(np.float32)
    M2 = np.where(col >= 128, M, NEG).astype(np.float32)
    d["masks_in"] = np.stack([M, M1, M2])
    d["_M"] = (M, M1, M2)
    return d


def rot_tables(positions):
    n = 8192
    half = np.arange(0, DH, 2, dtype=np.float32)
    inv_freq = (1.0 / (10000.0 ** (half / DH))).astype(np.float32)
    t = positions.astype(np.float32)
    freqs = (t[:, None] * inv_freq[None, :]).astype(np.float32)
    freqs = np.concatenate([freqs, freqs], -1)
    base_scale = ((half + 0.4 * DH) / (1.4 * DH)).astype(np.float64)
    power = ((t.astype(np.float64) - n // 2) / (WIN // 2))
    scale = (base_scale[None, :] ** power[:, None]).astype(np.float32)
    scale = np.concatenate([scale, scale], -1)
    cos = np.cos(freqs.astype(np.float64)).astype(np.float32)
    sin = np.sin(freqs.astype(np.float64)).astype(np.float32)
    sgn = np.ones(DH, dtype=np.float32)
    sgn[0:32] = -1.0
    inv = (1.0 / scale).astype(np.float32)
    return np.stack([cos * scale, sin * scale * sgn[None, :],
                     cos * inv, sin * inv * sgn[None, :]], axis=1).astype(np.float32)


def prep_A_core(ins, shared, b, h):
    d = {}
    x = ins["L0_attn_x"][b]
    slab = np.zeros((S, D), dtype=np.float32)
    lo = h * REALT - PADT
    src_lo = max(lo, 0)
    slab[src_lo - lo:, :] = x[src_lo:h * REALT + REALT, :]
    d["x_slab"] = slab
    d["rotp"] = rot_tables(np.arange(lo, lo + S))
    tm = np.ones(S, dtype=np.float32)
    if h == 0:
        tm[0:PADT] = 0.0
    d["tokm_in"] = np.ascontiguousarray(tm.reshape(NCH, 128).T)
    M, M1, M2 = shared["_M"]
    d["am_in"] = np.stack([M1, M2]) if h == 0 else np.stack([M, M])
    return d


def build_B(dump=None):
    nc = bass.Bass("TRN2", target_bir_lowering=False)
    stack = ExitStack()
    dt = nc.dram_tensor

    x1_in = dt("x1_in", [REALT, D], F32, kind="ExternalInput").ap()
    wkT = dt("wkT", [D, D], F32, kind="ExternalInput").ap()
    bkT = dt("bkT", [128, 8], F32, kind="ExternalInput").ap()
    bk_r = dt("bk_r", [1, D], F32, kind="ExternalInput").ap()
    wvT = dt("wvT", [D, D], F32, kind="ExternalInput").ap()
    bv_r = dt("bv_r", [1, D], F32, kind="ExternalInput").ap()
    wrf_in = dt("wrf_in", [16, DH, DH], F32, kind="ExternalInput").ap()
    eye_in = dt("eye_in", [128, 128], F32, kind="ExternalInput").ap()

    stats_out = dt("stats", [16, 66, 72], F32, kind="ExternalOutput").ap()
    dump_lg = dump_kfa = None
    if dump == "lg":
        dump_lg = dt("dbg_lg", [REALT, DH], F32, kind="ExternalOutput").ap()
    if dump == "kfa":
        dump_kfa = dt("dbg_kfa", [128, 8, REALT], F32, kind="ExternalOutput").ap()

    nmscr = dt("nmscr", [16, 1], F32).ap()

    NT = REALT // 512
    NLC = REALT // 128

    st = nc.sbuf_tensor
    with stack:
        kfa = stack.enter_context(st("kfa", [128, 8, REALT], BF16))
        vta = stack.enter_context(st("vta", [128, NLC, 1024], BF16))
        xnf = stack.enter_context(st("xnf", [128, 8, 512], BF16))
        wk_sb = stack.enter_context(st("wk_sb", [128, 8, 1024], BF16))
        wv_sb = stack.enter_context(st("wv_sb", [128, 8, 1024], BF16))
        w0_sb = stack.enter_context(st("w0_sb", [128, 16, DH], BF16))
        bk_sb = stack.enter_context(st("bk_sb", [128, 8], F32))
        bkr_sb = stack.enter_context(st("bkr_sb", [1, D], BF16))
        bv_sb = stack.enter_context(st("bv_sb", [1, D], BF16))
        xt32 = stack.enter_context(st("xt32", [128, 1024], F32))
        sqa = stack.enter_context(st("sqa", [128, 1024], BF16))
        qbt = stack.enter_context(st("qbt", [128, 1024], BF16))
        ktm = stack.enter_context(st("ktm", [128, 1024], BF16))
        sqk = stack.enter_context(st("sqk", [128, 1024], BF16))
        nrmall = stack.enter_context(st("nrmall", [128, NLC, 16], F32))
        lg32 = stack.enter_context(st("lg32", [128, DH], F32))
        ebf = stack.enter_context(st("ebf", [128, DH], BF16))
        mxall = stack.enter_context(st("mxall", [128, 16, NLC], F32))
        zall = stack.enter_context(st("zall", [128, 16, NLC], F32))
        red16 = stack.enter_context(st("red16", [128, 16], F32))
        tp16 = stack.enter_context(st("tp16", [16, 128], F32))
        gm16 = stack.enter_context(st("gm16", [16, 1], F32))
        zl16 = stack.enter_context(st("zl16", [16, 1], F32))
        nm128 = stack.enter_context(st("nm128", [128, 16], F32))
        nacc = stack.enter_context(st("nacc", [64, 16, 65], F32))
        onesc = stack.enter_context(st("onesc", [128, 1], BF16))
        onesr = stack.enter_context(st("onesr", [1, 128], BF16))
        eyeb = stack.enter_context(st("eyeb", [128, 128], BF16))
        eyef = stack.enter_context(st("eyef", [128, 128], F32))
        colA = stack.enter_context(st("colA", [128, 1], F32))
        colB = stack.enter_context(st("colB", [128, 1], F32))
        colC = stack.enter_context(st("colC", [128, 1], F32))
        colD = stack.enter_context(st("colD", [128, 1], F32))
        psA = stack.enter_context(nc.psum_tensor("psA", [128, 2048], F32))
        psB = stack.enter_context(nc.psum_tensor("psB", [128, 1024], F32))
        psN = stack.enter_context(nc.psum_tensor("psN", [128, 512], F32))
        psTb = stack.enter_context(nc.psum_tensor("psTb", [128, 1024], BF16))

        p = Prog(nc, stack)
        psAv = psA.rearrange("p (b n) -> p b n", b=4)

        xt32_b = Buf()
        sqa_b = Buf()
        qbt_b = Buf()
        xnf_b = Buf()
        kfa_b = Buf()
        vta_b = Buf()
        sqk_b = Buf()
        ktm_b = Buf()
        nrm_b = Buf()
        lg_b = Buf()
        ebf_b = Buf()
        mx_b = Buf()
        z_b = Buf()
        red_b = Buf()
        tp_b = Buf()
        gm_b = Buf()
        nm_b = Buf()
        nacc_b = Buf()
        bA = [Buf(), Buf(), Buf(), Buf()]
        bB = [Buf(), Buf()]
        bN = Buf()
        bT = Buf()
        col_b = [Buf(), Buf(), Buf(), Buf()]

        def emit_ln(src_ap, src_toks, out_ap, out_buf):
            t1 = p.emit("vector", lambda e: e.reduce_sum(colA[:], src_ap, axis=AX.X),
                        deps=src_toks + col_b[0].wdep())
            col_b[0].did_write(t1)
            t2 = p.emit("vector", lambda e: e.tensor_scalar_mul(colB[:], colA[:], -1.0 / D),
                        deps=[t1] + col_b[1].wdep())
            col_b[1].did_write(t2)
            col_b[0].did_read(t2)
            t3 = p.emit("scalar", lambda e: e.activation(
                sqa[:], src_ap, AF.Square, bias=colB[:], scale=1.0, accum_out=colC[:]),
                deps=src_toks + [t2] + sqa_b.wdep() + col_b[2].wdep())
            sqa_b.did_write(t3)
            col_b[2].did_write(t3)
            col_b[1].did_read(t3)
            t4 = p.emit("vector", lambda e: e.tensor_scalar(colC[:], colC[:], 1.0 / D, 1e-5, ALU.mult, ALU.add),
                        deps=[t3])
            col_b[2].did_write(t4)
            t5 = p.emit("scalar", lambda e: e.activation(colC[:], colC[:], AF.Sqrt), deps=[t4])
            col_b[2].did_write(t5)
            t6 = p.emit("vector", lambda e: e.reciprocal(colD[:], colC[:]), deps=[t5] + col_b[3].wdep())
            col_b[3].did_write(t6)
            col_b[2].did_read(t6)
            t7 = p.emit("vector", lambda e: e.tensor_scalar(
                out_ap, src_ap, colB[:], colD[:], ALU.add, ALU.mult),
                deps=src_toks + [t2, t6] + out_buf.wdep())
            out_buf.did_write(t7)
            col_b[1].did_read(t7)
            col_b[3].did_read(t7)
            sqa_b.did_read(t4)
            return t7

        wl = [p.dma("gpsimd", wk_sb[:], wkT.rearrange("(k p) n -> p k n", p=128)),
              p.dma("gpsimd", wv_sb[:], wvT.rearrange("(k p) n -> p k n", p=128)),
              p.dma("gpsimd", w0_sb[0:64], wrf_in.rearrange("h d f -> d h f")),
              p.dma("gpsimd", w0_sb[64:128], wrf_in.rearrange("h d f -> d h f")),
              p.dma("sync", bk_sb[:], bkT),
              p.dma("gpsimd", bkr_sb[:], bk_r),
              p.dma("gpsimd", bv_sb[:], bv_r),
              p.dma("gpsimd", eyeb[:], eye_in),
              p.dma("sync", eyef[:], eye_in)]
        t_ones = p.emit("gpsimd", lambda e: e.memset(onesc[:], 1.0))
        t_ones2 = p.emit("gpsimd", lambda e: e.memset(onesr[:], 1.0))
        tnz = p.emit("vector", lambda e: e.memset(nacc[:].rearrange("p a b -> p (a b)"), 0.0))
        nacc_b.did_write(tnz)
        wl += [t_ones, t_ones2]

        ent = p.fence()
        bT = Buf()

        def pe_tr(srcs, dst_view, deps, dst_buf):
            tts = None
            n = len(srcs)
            for i, sv in enumerate(srcs):
                tts = p.emit("tensor", lambda e, sv=sv, i=i: e.transpose(
                    psTb[:, i * 128:(i + 1) * 128], sv, eyeb[:]),
                    deps=(list(deps) + bT.wdep()) if i == 0 else [], inc=(i == n - 1))
            bT.did_write(tts)
            tev = p.emit("scalar", lambda e, n=n: e.copy(
                dst_view, psTb[:, 0:n * 128].rearrange("p (k m) -> p k m", k=n)),
                deps=[tts] + dst_buf.wdep())
            dst_buf.did_write(tev)
            bT.did_read(tev)
            return tts, tev

        for t in range(NT):
            ent = p.fence()
            r0 = t * 512
            ztl = []
            for c2 in range(4):
                chunk = 4 * t + c2
                tl = p.dma("sync", xt32[:], x1_in[chunk * 128:(chunk + 1) * 128, :],
                           deps=xt32_b.wdep() + ent)
                xt32_b.did_write(tl)
                t7 = emit_ln(xt32[:], [tl], qbt[:], qbt_b)
                xt32_b.did_read(t7)
                srcs = [qbt[:, k * 128:(k + 1) * 128] for k in range(8)]
                tts, tev = pe_tr(srcs, xnf[:, :, c2 * 128:c2 * 128 + 128],
                                 [t7] + ent + (xnf_b.wdep() if c2 == 0 else []), xnf_b)
                qbt_b.did_read(tts)
                ztl.append(tev)
            xnf_b.did_write(ztl[-1])
            for dc in range(8):
                bank = dc % 2
                bdeps = bA[bank].wdep() + ztl + wl + ent
                tm = None
                for k in range(8):
                    tm = p.emit("tensor", lambda e, dc=dc, k=k, bank=bank: e.matmul(
                        psAv[:, bank, :], wk_sb[:, k, dc * 128:dc * 128 + 128], xnf[:, k, :],
                        start=(k == 0), stop=(k == 7)),
                        deps=(bdeps if k == 0 else []), inc=(k == 7))
                bA[bank].did_write(tm)
                xnf_b.did_read(tm)
                te = p.emit("vector", lambda e, dc=dc, r0=r0, bank=bank: e.tensor_scalar_add(
                    kfa[:, dc, r0:r0 + 512], psAv[:, bank, :], bk_sb[:, dc:dc + 1]),
                    deps=[tm, wl[4]] + (kfa_b.wdep() if (t == 0 and dc == 0) else []))
                kfa_b.did_write(te)
                bA[bank].did_read(te)
            for tsn in range(4):
                for nt in range(2):
                    bank = 2 + nt
                    bdeps = bA[bank].wdep() + ztl + wl
                    tm = None
                    for k in range(8):
                        tm = p.emit("tensor", lambda e, tsn=tsn, nt=nt, k=k, bank=bank: e.matmul(
                            psAv[:, bank, :], xnf[:, k, tsn * 128:tsn * 128 + 128],
                            wv_sb[:, k, nt * 512:nt * 512 + 512],
                            start=(k == 0), stop=False),
                            deps=(bdeps if k == 0 else []), inc=False)
                    tm = p.emit("tensor", lambda e, nt=nt, bank=bank: e.matmul(
                        psAv[:, bank, :], onesr[:], bv_sb[:, nt * 512:nt * 512 + 512],
                        start=False, stop=True), inc=True)
                    bA[bank].did_write(tm)
                    xnf_b.did_read(tm)
                    lc = 4 * t + tsn
                    te = p.emit("vector", lambda e, lc=lc, nt=nt, bank=bank: e.tensor_copy(
                        vta[:, lc, nt * 512:nt * 512 + 512], psAv[:, bank, :]),
                        deps=[tm] + (vta_b.wdep() if (lc == 0 and nt == 0) else []))
                    vta_b.did_write(te)
                    bA[bank].did_read(te)
            for tsn in range(4):
                lc = 4 * t + tsn
                for nt in range(2):
                    bank = 2 + nt
                    bdeps = bA[bank].wdep() + ztl + wl
                    tm = None
                    for k in range(8):
                        tm = p.emit("tensor", lambda e, tsn=tsn, nt=nt, k=k, bank=bank: e.matmul(
                            psAv[:, bank, :], xnf[:, k, tsn * 128:tsn * 128 + 128],
                            wk_sb[:, k, nt * 512:nt * 512 + 512],
                            start=(k == 0), stop=False),
                            deps=(bdeps if k == 0 else []), inc=False)
                    tm = p.emit("tensor", lambda e, nt=nt, bank=bank: e.matmul(
                        psAv[:, bank, :], onesr[:], bkr_sb[:, nt * 512:nt * 512 + 512],
                        start=False, stop=True), inc=True)
                    bA[bank].did_write(tm)
                    xnf_b.did_read(tm)
                    te = p.emit("vector", lambda e, nt=nt, bank=bank: e.tensor_copy(
                        ktm[:, nt * 512:nt * 512 + 512], psAv[:, bank, :]),
                        deps=[tm] + (ktm_b.wdep() if nt == 0 else []))
                    ktm_b.did_write(te)
                    bA[bank].did_read(te)
                tsq = p.emit("scalar", lambda e: e.activation(sqk[:], ktm[:], AF.Square),
                             deps=[ktm_b.w] + sqk_b.wdep())
                sqk_b.did_write(tsq)
                ktm_b.did_read(tsq)
                trd = p.emit("vector", lambda e, lc=lc: e.reduce_sum(
                    nrmall[:, lc, :], sqk[:].rearrange("p (h d) -> p h d", h=16), axis=AX.X),
                    deps=[tsq] + (nrm_b.wdep() if lc == 0 else []))
                nrm_b.did_write(trd)
                sqk_b.did_read(trd)
        th = p.emit("vector", lambda e: e.tensor_scalar_mul(
            nrmall[:].rearrange("p a b -> p (a b)"), nrmall[:].rearrange("p a b -> p (a b)"), -0.5),
            deps=[nrm_b.w])
        nrm_b.did_write(th)

        for h in range(16):
            for lc in range(NLC):
                hp, half = h // 2, h % 2
                bank = lc % 2
                tm = p.emit("tensor", lambda e, h=h, lc=lc, hp=hp, half=half, bank=bank: e.matmul(
                    psB[:, bank * 512:bank * 512 + 64],
                    kfa[64 * half:64 * half + 64, hp, lc * 128:lc * 128 + 128],
                    w0_sb[64 * half:64 * half + 64, h, :], start=True, stop=True),
                    deps=[kfa_b.w, wl[2], wl[3]] + bB[bank].wdep(), inc=True)
                bB[bank].did_write(tm)
                kfa_b.did_read(tm)
                tev = p.emit("vector", lambda e, lc=lc, h=h, bank=bank: e.tensor_scalar_add(
                    lg32[:], psB[:, bank * 512:bank * 512 + 64], nrmall[:, lc, h:h + 1]),
                    deps=[tm, th] + lg_b.wdep())
                lg_b.did_write(tev)
                nrm_b.did_read(tev)
                bB[bank].did_read(tev)
                tmx = p.emit("vector", lambda e, h=h, lc=lc: e.reduce_max(
                    mxall[:, h, lc:lc + 1], lg32[:], axis=AX.X),
                    deps=[tev] + (mx_b.wdep() if (h == 0 and lc == 0) else []))
                mx_b.did_write(tmx)
                lg_b.did_read(tmx)
                if dump_lg is not None and h == 0:
                    tds = p.dma("sync", dump_lg[lc * 128:(lc + 1) * 128, :], lg32[:], deps=[tev])
                    lg_b.did_read(tds)
        t1 = p.emit("vector", lambda e: e.reduce_max(red16[:], mxall[:], axis=AX.X),
                    deps=[mx_b.w] + red_b.wdep())
        red_b.did_write(t1)
        mx_b.did_read(t1)
        t2 = p.emit("tensor", lambda e: e.transpose(psN[0:16, 0:128], red16[:], eyef[:]),
                    deps=[t1, wl[-1]] + bN.wdep(), inc=True)
        bN.did_write(t2)
        red_b.did_read(t2)
        t3 = p.emit("vector", lambda e: e.tensor_copy(tp16[:], psN[0:16, 0:128]),
                    deps=[t2] + tp_b.wdep())
        tp_b.did_write(t3)
        bN.did_read(t3)
        t4 = p.emit("vector", lambda e: e.reduce_max(gm16[:], tp16[:], axis=AX.X),
                    deps=[t3] + gm_b.wdep())
        gm_b.did_write(t4)
        tp_b.did_read(t4)
        tsg = p.dma("sync", nmscr[:], gm16[:], deps=[t4])
        gm_b.did_read(tsg)
        tldm = p.dma("sync", nm128[:], nmscr.transpose([1, 0]).broadcast_to([128, 16]),
                     deps=[tsg] + nm_b.wdep())
        tnm = p.emit("vector", lambda e: e.tensor_scalar_mul(nm128[:], nm128[:], -1.0), deps=[tldm])
        nm_b.did_write(tnm)

        for h in range(16):
            for lc in range(NLC):
                hp, half = h // 2, h % 2
                bank = lc % 2
                tm = p.emit("tensor", lambda e, h=h, lc=lc, hp=hp, half=half, bank=bank: e.matmul(
                    psB[:, bank * 512:bank * 512 + 64],
                    kfa[64 * half:64 * half + 64, hp, lc * 128:lc * 128 + 128],
                    w0_sb[64 * half:64 * half + 64, h, :], start=True, stop=True),
                    deps=[kfa_b.w] + bB[bank].wdep(), inc=True)
                bB[bank].did_write(tm)
                kfa_b.did_read(tm)
                tev = p.emit("vector", lambda e, lc=lc, h=h, bank=bank: e.tensor_scalar_add(
                    lg32[:], psB[:, bank * 512:bank * 512 + 64], nrmall[:, lc, h:h + 1]),
                    deps=[tm, tnm] + lg_b.wdep())
                lg_b.did_write(tev)
                nrm_b.did_read(tev)
                bB[bank].did_read(tev)
                tex = p.emit("scalar", lambda e, h=h, lc=lc: e.activation(
                    ebf[:], lg32[:], AF.Exp, bias=nm128[:, h:h + 1], scale=1.0,
                    accum_out=zall[:, h, lc:lc + 1]),
                    deps=[tev, tnm] + ebf_b.wdep() + (z_b.wdep() if (h == 0 and lc == 0) else []))
                ebf_b.did_write(tex)
                z_b.did_write(tex)
                lg_b.did_read(tex)
                nm_b.did_read(tex)
                av1 = p.emit("tensor", lambda e, h=h, lc=lc: e.matmul(
                    psN[0:64, 0:64], ebf[:], vta[:, lc, h * 64:h * 64 + 64],
                    start=True, stop=False),
                    deps=[tex, vta_b.w] + bN.wdep(), inc=False)
                av2 = p.emit("tensor", lambda e, h=h, lc=lc: e.matmul(
                    psN[0:64, 64:65], ebf[:], onesc[:],
                    start=True, stop=True), inc=True)
                bN.did_write(av2)
                ebf_b.did_read(av2)
                vta_b.did_read(av1)
                tacc = p.emit("vector", lambda e, h=h: e.tensor_add(
                    nacc[:, h, :], nacc[:, h, :], psN[0:64, 0:65]),
                    deps=[av2] + nacc_b.wdep())
                nacc_b.did_write(tacc)
                bN.did_read(tacc)

        t1 = p.emit("vector", lambda e: e.reduce_sum(red16[:], zall[:], axis=AX.X),
                    deps=[z_b.w] + red_b.wdep())
        red_b.did_write(t1)
        z_b.did_read(t1)
        t2 = p.emit("tensor", lambda e: e.transpose(psN[0:16, 0:128], red16[:], eyef[:]),
                    deps=[t1] + bN.wdep(), inc=True)
        bN.did_write(t2)
        red_b.did_read(t2)
        t3 = p.emit("vector", lambda e: e.tensor_copy(tp16[:], psN[0:16, 0:128]),
                    deps=[t2] + tp_b.wdep())
        tp_b.did_write(t3)
        bN.did_read(t3)
        t4 = p.emit("vector", lambda e: e.reduce_sum(zl16[:], tp16[:], axis=AX.X),
                    deps=[t3])
        tp_b.did_read(t4)

        for h in range(16):
            tso = p.dma("sync", stats_out[h, 0:64, 0:65], nacc[:, h, :], deps=[nacc_b.w])
            nacc_b.did_read(tso)
        ts1 = p.dma("sync", stats_out[:, 64:65, 0:1], gm16[:].unsqueeze(1), deps=[t4, gm_b.w],
                    allow_slow_non_contiguous=True)
        ts2 = p.dma("sync", stats_out[:, 64:65, 1:2], zl16[:].unsqueeze(1), deps=[t4],
                    allow_slow_non_contiguous=True)
        gm_b.did_read(ts1)

        with nc.Block() as blk:
            p.finalize(blk)
    return nc


def build_C(dump=None):
    nc = bass.Bass("TRN2", target_bir_lowering=False)
    stack = ExitStack()
    dt = nc.dram_tensor

    x1_in = dt("x1_in", [REALT, D], F32, kind="ExternalInput").ap()
    stats_in = dt("stats_in", [2, 16, 66, 72], F32, kind="ExternalInput").ap()
    wqT = dt("wqT", [D, D], F32, kind="ExternalInput").ap()
    bqT = dt("bqT", [128, 8], F32, kind="ExternalInput").ap()
    woT = dt("woT", [D, D], F32, kind="ExternalInput").ap()
    bo_r = dt("bo_r", [1, D], F32, kind="ExternalInput").ap()
    wrf_in = dt("wrf_in", [16, DH, DH], F32, kind="ExternalInput").ap()
    wgT = dt("wgT", [D, 8], F32, kind="ExternalInput").ap()
    win_in = dt("win_in", [8, D, 4096], F32, kind="ExternalInput").ap()
    wout_in = dt("wout_in", [8, 4096, D], F32, kind="ExternalInput").ap()
    eye_in = dt("eye_in", [128, 128], F32, kind="ExternalInput").ap()

    out_t = dt("out", [REALT, D], F32, kind="ExternalOutput").ap()
    dump_map = {}
    for nm in ("ctx", "a1", "gates", "f1"):
        if dump == nm:
            dump_map[nm] = dt("dbg_" + nm, [REALT, D if nm != "gates" else 8], F32,
                              kind="ExternalOutput").ap()

    xnb = dt("xnb", [REALT, D], BF16).ap()
    f1acc = dt("f1acc", [REALT, D], F32).ap()
    sgscr = dt("sgscr", [16, 66, 72], F32).ap()
    ezscr = dt("ezscr", [16, 1], F32).ap()
    xgscr = dt("xgscr", [512, D], BF16).ap()

    NT = REALT // 512
    NLC = REALT // 128
    NJ = 32

    st = nc.sbuf_tensor
    with stack:
        arena = stack.enter_context(st("arena", [128, 65536], BF16))
        xnf = stack.enter_context(st("xnf", [128, 8, 512], BF16))
        ctx_sb = stack.enter_context(st("ctx_sb", [128, 8, 512], BF16))
        hmoe = stack.enter_context(st("hmoe", [128, NJ, 512], BF16))
        w0_sb = stack.enter_context(st("w0_sb", [128, 16, DH], BF16))
        bq_sb = stack.enter_context(st("bq_sb", [128, 8], F32))
        bo_sb = stack.enter_context(st("bo_sb", [1, D], BF16))
        wg_sb = stack.enter_context(st("wg_sb", [128, 8, 8], BF16))
        lhsT_sb = stack.enter_context(st("lhsT_sb", [128, 16, 66], BF16))
        ccol = stack.enter_context(st("ccol", [16, 4], F32))
        xt32 = stack.enter_context(st("xt32", [128, 1024], F32))
        sqa = stack.enter_context(st("sqa", [128, 1024], BF16))
        qbt = stack.enter_context(st("qbt", [128, 1024], BF16))
        phib = stack.enter_context(st("phib", [128, DH], BF16))
        phiT = stack.enter_context(st("phiT", [65, 128], BF16))
        numsb = stack.enter_context(st("numsb", [64, 128], BF16))
        denr = stack.enter_context(st("denr", [1, 128], F32))
        denb = stack.enter_context(st("denb", [1, 128], BF16))
        gat = stack.enter_context(st("gat", [128, NLC, 8], F32))
        g8 = stack.enter_context(st("g8", [128, 8], F32))
        yt32 = stack.enter_context(st("yt32", [128, 1024], F32))
        eyeb = stack.enter_context(st("eyeb", [128, 128], BF16))
        onesc = stack.enter_context(st("onesc", [128, 1], BF16))
        onesr = stack.enter_context(st("onesr", [1, 128], BF16))
        colA = stack.enter_context(st("colA", [128, 1], F32))
        colB = stack.enter_context(st("colB", [128, 1], F32))
        colC = stack.enter_context(st("colC", [128, 1], F32))
        colD = stack.enter_context(st("colD", [128, 1], F32))
        psA = stack.enter_context(nc.psum_tensor("psA", [128, 2048], F32))
        psB = stack.enter_context(nc.psum_tensor("psB", [128, 1024], F32))
        psO = stack.enter_context(nc.psum_tensor("psO", [128, 512], F32))
        psTb = stack.enter_context(nc.psum_tensor("psTb", [128, 1024], BF16))

        p = Prog(nc, stack)
        psAv = psA.rearrange("p (b n) -> p b n", b=4)
        AR_wq = arena[:, 0:8192].rearrange("p (k n) -> p k n", k=8)
        AR_wo = arena[:, 8192:16384].rearrange("p (k n) -> p k n", k=8)
        AR_win = arena[:, 0:32768].rearrange("p (k n) -> p k n", k=8)
        AR_wout = arena[:, 32768:65536].rearrange("p (k n) -> p k n", k=NJ)
        sA = arena[0:16, 16384:25888].bitcast(F32).rearrange("p (a b) -> p a b", a=66)
        sB2 = arena[0:16, 25888:35392].bitcast(F32).rearrange("p (a b) -> p a b", a=66)

        arena_b = Buf()
        xnf_b = Buf()
        ctx_b = Buf()
        hmoe_b = Buf()
        xt32_b = Buf()
        sqa_b = Buf()
        qbt_b = Buf()
        phib_b = Buf()
        phiT_b = Buf()
        num_b = Buf()
        den_b = Buf()
        gat_b = Buf()
        g8_b = Buf()
        yt_b = Buf()
        sA_b = Buf()
        sB_b = Buf()
        cc_b = Buf()
        lh_b = Buf()
        bA = [Buf(), Buf(), Buf(), Buf()]
        bB = [Buf(), Buf()]
        bO = Buf()
        bT = Buf()
        col_b = [Buf(), Buf(), Buf(), Buf()]
        f1_b = [Buf() for _ in range(NT)]

        def emit_ln(src_ap, src_toks, out_ap, out_buf):
            t1 = p.emit("vector", lambda e: e.reduce_sum(colA[:], src_ap, axis=AX.X),
                        deps=src_toks + col_b[0].wdep())
            col_b[0].did_write(t1)
            t2 = p.emit("vector", lambda e: e.tensor_scalar_mul(colB[:], colA[:], -1.0 / D),
                        deps=[t1] + col_b[1].wdep())
            col_b[1].did_write(t2)
            col_b[0].did_read(t2)
            t3 = p.emit("scalar", lambda e: e.activation(
                sqa[:], src_ap, AF.Square, bias=colB[:], scale=1.0, accum_out=colC[:]),
                deps=src_toks + [t2] + sqa_b.wdep() + col_b[2].wdep())
            sqa_b.did_write(t3)
            col_b[2].did_write(t3)
            col_b[1].did_read(t3)
            t4 = p.emit("vector", lambda e: e.tensor_scalar(colC[:], colC[:], 1.0 / D, 1e-5, ALU.mult, ALU.add),
                        deps=[t3])
            col_b[2].did_write(t4)
            t5 = p.emit("scalar", lambda e: e.activation(colC[:], colC[:], AF.Sqrt), deps=[t4])
            col_b[2].did_write(t5)
            t6 = p.emit("vector", lambda e: e.reciprocal(colD[:], colC[:]), deps=[t5] + col_b[3].wdep())
            col_b[3].did_write(t6)
            col_b[2].did_read(t6)
            t7 = p.emit("vector", lambda e: e.tensor_scalar(
                out_ap, src_ap, colB[:], colD[:], ALU.add, ALU.mult),
                deps=src_toks + [t2, t6] + out_buf.wdep())
            out_buf.did_write(t7)
            col_b[1].did_read(t7)
            col_b[3].did_read(t7)
            sqa_b.did_read(t4)
            return t7

        wl = [p.dma("gpsimd", AR_wq[:, :, :], wqT.rearrange("(k p) n -> p k n", p=128)),
              p.dma("gpsimd", AR_wo[:, :, :], woT.rearrange("(k p) n -> p k n", p=128)),
              p.dma("gpsimd", w0_sb[0:64], wrf_in.rearrange("h d f -> d h f")),
              p.dma("gpsimd", w0_sb[64:128], wrf_in.rearrange("h d f -> d h f")),
              p.dma("sync", bq_sb[:], bqT),
              p.dma("gpsimd", bo_sb[:], bo_r),
              p.dma("gpsimd", wg_sb[:], wgT.rearrange("(k p) n -> p k n", p=128)),
              p.dma("gpsimd", eyeb[:], eye_in)]
        arena_b.did_write(wl[0])
        arena_b.did_write(wl[1])
        t_ones = p.emit("gpsimd", lambda e: e.memset(onesc[:], 1.0))
        t_ones2 = p.emit("gpsimd", lambda e: e.memset(onesr[:], 1.0))
        t_phi1 = p.emit("gpsimd", lambda e: e.memset(phiT[64:65, :], 1.0))
        wl += [t_ones, t_ones2]

        tA = p.dma("sync", sA[:], stats_in[0], deps=sA_b.wdep())
        tB = p.dma("sync", sB2[:], stats_in[1], deps=sB_b.wdep())
        sA_b.did_write(tA)
        sB_b.did_write(tB)
        mgA = sA[:, 64, 0:1]
        mgB = sB2[:, 64, 0:1]
        t1 = p.emit("vector", lambda e: e.tensor_tensor(ccol[:, 0:1], mgA, mgB, ALU.max),
                    deps=[tA, tB] + cc_b.wdep())
        t2 = p.emit("vector", lambda e: e.tensor_scalar_mul(ccol[:, 1:2], ccol[:, 0:1], -1.0),
                    deps=[t1])
        t3 = p.emit("scalar", lambda e: e.activation(ccol[:, 2:3], mgA, AF.Exp, bias=ccol[:, 1:2], scale=1.0),
                    deps=[t2, tA])
        t4 = p.emit("scalar", lambda e: e.activation(ccol[:, 3:4], mgB, AF.Exp, bias=ccol[:, 1:2], scale=1.0),
                    deps=[t2, tB])
        cc_b.did_write(t4)
        sAf = sA[:].rearrange("p a b -> p (a b)")
        sBf = sB2[:].rearrange("p a b -> p (a b)")
        t5 = p.emit("vector", lambda e: e.tensor_scalar_mul(sBf, sBf, ccol[:, 3:4]), deps=[t4, tB])
        sB_b.did_write(t5)
        t6 = p.emit("vector", lambda e: e.scalar_tensor_tensor(
            sAf, sAf, ccol[:, 2:3], sBf, ALU.mult, ALU.add), deps=[t3, t5, tA])
        sA_b.did_write(t6)
        sB_b.did_read(t6)
        cc_b.did_read(t6)
        t7 = p.emit("vector", lambda e: e.tensor_copy(sA[:, 64, 0:1], ccol[:, 0:1]), deps=[t6, t1])
        t8 = p.emit("vector", lambda e: e.tensor_scalar_mul(ccol[:, 1:2], sA[:, 64, 1:2], 1e-6),
                    deps=[t6])
        cc_b.did_write(t8)
        sA_b.did_write(t7)
        ts1 = p.dma("sync", sgscr[:], sA[:], deps=[t7, t6])
        ts2 = p.dma("sync", ezscr[:], ccol[:, 1:2], deps=[t8])
        sA_b.did_read(ts1)
        cc_b.did_read(ts2)
        tz = p.emit("gpsimd", lambda e: e.memset(lhsT_sb[64:65, :, :].rearrange("p a b -> p (a b)"), 0.0))
        lh_b.did_write(tz)
        tl1 = None
        for h in range(16):
            tl1 = p.dma("gpsimd", lhsT_sb[0:64, h, 0:65], sgscr[h, 0:64, 0:65], deps=[ts1])
        tl2 = p.dma("gpsimd", lhsT_sb[64:65, :, 64:65], ezscr.transpose([1, 0]).unsqueeze(2), deps=[ts2, tz],
                    allow_slow_non_contiguous=True)
        lh_b.did_write(tl1)
        lh_b.did_write(tl2)
        lh_dep = [tl1, tl2, tz]

        for c in range(NLC):
            tl = p.dma("sync", xt32[:], x1_in[c * 128:(c + 1) * 128, :], deps=xt32_b.wdep())
            xt32_b.did_write(tl)
            t7 = emit_ln(xt32[:], [tl], qbt[:], qbt_b)
            xt32_b.did_read(t7)
            ts_ = p.dma("sync", xnb[c * 128:(c + 1) * 128, :], qbt[:], deps=[t7])
            qbt_b.did_read(ts_)

        ent = p.fence()

        for t in range(NT):
            r0 = t * 512
            ztl = []
            for k in range(8):
                tt = p.dmat("scalar", xnf[:, k, :], xnb[r0:r0 + 512, k * 128:k * 128 + 128],
                            deps=ent + (xnf_b.wdep() if k == 0 else []))
                ztl.append(tt)
            xnf_b.did_write(ztl[-1])
            qf = hmoe
            for dc in range(8):
                bank = dc % 2
                bdeps = bA[bank].wdep() + ztl + wl
                tm = None
                for k in range(8):
                    tm = p.emit("tensor", lambda e, dc=dc, k=k, bank=bank: e.matmul(
                        psAv[:, bank, :], AR_wq[:, k, dc * 128:dc * 128 + 128], xnf[:, k, :],
                        start=(k == 0), stop=(k == 7)),
                        deps=(bdeps if k == 0 else []), inc=(k == 7))
                bA[bank].did_write(tm)
                xnf_b.did_read(tm)
                arena_b.did_read(tm)
                te = p.emit("vector", lambda e, dc=dc, bank=bank: e.tensor_scalar_add(
                    qf[:, dc, :], psAv[:, bank, :], bq_sb[:, dc:dc + 1]),
                    deps=[tm, wl[4]] + (hmoe_b.wdep() if dc == 0 else []))
                hmoe_b.did_write(te)
                bA[bank].did_read(te)
            qf_tok = hmoe_b.w
            for lq in range(4):
                lc = 4 * t + lq
                for h in range(16):
                    hp, half = h // 2, h % 2
                    bank = h % 2
                    tm = p.emit("tensor", lambda e, h=h, lq=lq, hp=hp, half=half, bank=bank: e.matmul(
                        psB[:, bank * 512:bank * 512 + 64],
                        qf[64 * half:64 * half + 64, hp, lq * 128:lq * 128 + 128],
                        w0_sb[64 * half:64 * half + 64, h, :], start=True, stop=True),
                        deps=[qf_tok, wl[2], wl[3]] + bB[bank].wdep(), inc=True)
                    bB[bank].did_write(tm)
                    hmoe_b.did_read(tm)
                    tmx = p.emit("vector", lambda e, bank=bank: e.reduce_max(
                        colA[:], psB[:, bank * 512:bank * 512 + 64], axis=AX.X, negate=True),
                        deps=[tm] + col_b[0].wdep())
                    col_b[0].did_write(tmx)
                    tex = p.emit("scalar", lambda e, bank=bank: e.activation(
                        sqa[:].bitcast(F32)[:, 0:DH], psB[:, bank * 512:bank * 512 + 64],
                        AF.Exp, bias=colA[:], scale=1.0, accum_out=colB[:]),
                        deps=[tmx, tm] + sqa_b.wdep() + col_b[1].wdep())
                    sqa_b.did_write(tex)
                    col_b[1].did_write(tex)
                    col_b[0].did_read(tex)
                    bB[bank].did_read(tex)
                    trc = p.emit("vector", lambda e: e.reciprocal(colC[:], colB[:]),
                                 deps=[tex] + col_b[2].wdep())
                    col_b[2].did_write(trc)
                    col_b[1].did_read(trc)
                    tph = p.emit("vector", lambda e: e.tensor_scalar_mul(
                        phib[:], sqa[:].bitcast(F32)[:, 0:DH], colC[:]),
                        deps=[trc] + phib_b.wdep())
                    phib_b.did_write(tph)
                    sqa_b.did_read(tph)
                    col_b[2].did_read(tph)
                    tt = p.emit("tensor", lambda e: e.transpose(psTb[0:64, 0:128], phib[:], eyeb[:]),
                                deps=[tph, wl[7]] + bT.wdep(), inc=True)
                    bT.did_write(tt)
                    phib_b.did_read(tt)
                    tpc = p.emit("scalar", lambda e: e.copy(phiT[0:64, :], psTb[0:64, 0:128]),
                                 deps=[tt, t_phi1] + phiT_b.wdep())
                    phiT_b.did_write(tpc)
                    bT.did_read(tpc)
                    tmo = p.emit("tensor", lambda e, h=h: e.matmul(
                        psO[0:65, 0:128], lhsT_sb[0:65, h, 0:65], phiT[0:65, :],
                        start=True, stop=True),
                        deps=[tpc] + lh_dep + bO.wdep(), inc=True)
                    bO.did_write(tmo)
                    phiT_b.did_read(tmo)
                    lh_b.did_read(tmo)
                    tde = p.emit("vector", lambda e: e.tensor_copy(denr[:], psO[64:65, 0:128]),
                                 deps=[tmo] + den_b.wdep())
                    tdr = p.emit("vector", lambda e: e.reciprocal(denr[:], denr[:]), deps=[tde])
                    tdb = p.emit("vector", lambda e: e.tensor_copy(denb[:], denr[:]), deps=[tdr])
                    den_b.did_write(tdb)
                    tnu = p.emit("scalar", lambda e: e.copy(numsb[:], psO[0:64, 0:128]),
                                 deps=[tmo] + num_b.wdep())
                    num_b.did_write(tnu)
                    bO.did_read(tde)
                    bO.did_read(tnu)
                    obank = 2
                    tbc = p.emit("tensor", lambda e: e.matmul(
                        psAv[0:64, obank, 0:128], onesr[:, 0:64], denb[:],
                        start=True, stop=True),
                        deps=[tdb, t_ones2] + bA[obank].wdep(), inc=True)
                    bA[obank].did_write(tbc)
                    den_b.did_read(tbc)
                    tcx = p.emit("vector", lambda e, h=h, hp=hp, half=half, lq=lq: e.tensor_tensor(
                        ctx_sb[64 * half:64 * half + 64, hp, lq * 128:lq * 128 + 128],
                        numsb[:], psAv[0:64, obank, 0:128], ALU.mult),
                        deps=[tbc, tnu] + (ctx_b.wdep() if (h == 0 and lq == 0) else []))
                    ctx_b.did_write(tcx)
                    num_b.did_read(tcx)
                    bA[obank].did_read(tcx)
            for tsn in range(4):
                lc = 4 * t + tsn
                for nt in range(2):
                    bank = 3
                    bdeps = bA[bank].wdep() + [ctx_b.w] + wl
                    tm = None
                    for k in range(8):
                        tm = p.emit("tensor", lambda e, tsn=tsn, nt=nt, k=k, bank=bank: e.matmul(
                            psAv[:, bank, :], ctx_sb[:, k, tsn * 128:tsn * 128 + 128],
                            AR_wo[:, k, nt * 512:nt * 512 + 512],
                            start=(k == 0), stop=False),
                            deps=(bdeps if k == 0 else []), inc=False)
                    tm = p.emit("tensor", lambda e, nt=nt, bank=bank: e.matmul(
                        psAv[:, bank, :], onesr[:], bo_sb[:, nt * 512:nt * 512 + 512],
                        start=False, stop=True), inc=True)
                    bA[bank].did_write(tm)
                    ctx_b.did_read(tm)
                    arena_b.did_read(tm)
                    te = p.emit("vector", lambda e, nt=nt, bank=bank: e.tensor_copy(
                        yt32[:, nt * 512:nt * 512 + 512], psAv[:, bank, :]),
                        deps=[tm] + (yt_b.wdep() if nt == 0 else []))
                    yt_b.did_write(te)
                    bA[bank].did_read(te)
                tso = p.dma("sync", f1acc[lc * 128:(lc + 1) * 128, :], yt32[:], deps=[yt_b.w] + f1_b[t].wdep())
                yt_b.did_read(tso)
                f1_b[t].did_write(tso)
            for tsn in range(4):
                lc = 4 * t + tsn
                bank = 1
                bdeps = bB[bank].wdep() + ztl + wl
                tm = None
                for k in range(8):
                    tm = p.emit("tensor", lambda e, tsn=tsn, k=k, bank=bank: e.matmul(
                        psB[:, bank * 512:bank * 512 + 8], xnf[:, k, tsn * 128:tsn * 128 + 128],
                        wg_sb[:, k, :], start=(k == 0), stop=(k == 7)),
                        deps=(bdeps if k == 0 else []), inc=(k == 7))
                bB[bank].did_write(tm)
                xnf_b.did_read(tm)
                tg1 = p.emit("vector", lambda e, bank=bank: e.tensor_copy(g8[:], psB[:, bank * 512:bank * 512 + 8]),
                             deps=[tm] + g8_b.wdep())
                bB[bank].did_read(tg1)
                tg2 = p.emit("vector", lambda e: e.reduce_max(colA[:], g8[:], axis=AX.X),
                             deps=[tg1] + col_b[0].wdep())
                col_b[0].did_write(tg2)
                tg3 = p.emit("vector", lambda e: e.tensor_scalar_sub(g8[:], g8[:], colA[:]), deps=[tg2])
                col_b[0].did_read(tg3)
                tg4 = p.emit("vector", lambda e: e.tensor_scalar(
                    sqa[:].bitcast(F32)[:, 0:8], g8[:], 0.0, NEG, ALU.is_equal, ALU.mult),
                    deps=[tg3] + sqa_b.wdep())
                sqa_b.did_write(tg4)
                tg5 = p.emit("vector", lambda e: e.tensor_add(
                    sqa[:].bitcast(F32)[:, 8:16], g8[:], sqa[:].bitcast(F32)[:, 0:8]),
                    deps=[tg4])
                sqa_b.did_write(tg5)
                tg6 = p.emit("vector", lambda e: e.reduce_max(
                    colB[:], sqa[:].bitcast(F32)[:, 8:16], axis=AX.X),
                    deps=[tg5] + col_b[1].wdep())
                col_b[1].did_write(tg6)
                sqa_b.did_read(tg6)
                tg7 = p.emit("vector", lambda e: e.tensor_scalar(
                    sqa[:].bitcast(F32)[:, 0:8], g8[:], colB[:], None, ALU.is_ge),
                    deps=[tg6, tg3])
                sqa_b.did_write(tg7)
                col_b[1].did_read(tg7)
                tg8 = p.emit("scalar", lambda e: e.activation(g8[:], g8[:], AF.Exp), deps=[tg7, tg3])
                g8_b.did_write(tg8)
                tg9 = p.emit("vector", lambda e: e.tensor_tensor(
                    g8[:], g8[:], sqa[:].bitcast(F32)[:, 0:8], ALU.mult), deps=[tg8])
                sqa_b.did_read(tg9)
                tg10 = p.emit("vector", lambda e: e.reduce_sum(colC[:], g8[:], axis=AX.X),
                              deps=[tg9] + col_b[2].wdep())
                col_b[2].did_write(tg10)
                tg11 = p.emit("vector", lambda e: e.reciprocal(colC[:], colC[:]), deps=[tg10])
                col_b[2].did_write(tg11)
                tg12 = p.emit("vector", lambda e, lc=lc: e.tensor_scalar_mul(
                    gat[:, lc, :], g8[:], colC[:]),
                    deps=[tg11] + (gat_b.wdep() if lc == 0 else []))
                gat_b.did_write(tg12)
                g8_b.did_write(tg12)
                col_b[2].did_read(tg12)

        if "gates" in dump_map:
            entg = p.fence()
            for lc in range(NLC):
                tso = p.dma("sync", dump_map["gates"][lc * 128:(lc + 1) * 128, :],
                            gat[:, lc, :], deps=entg)
                gat_b.did_read(tso)
        if "a1" in dump_map:
            entg = p.fence()
            for lc in range(NLC):
                tl = p.dma("sync", yt32[:], f1acc[lc * 128:(lc + 1) * 128, :], deps=entg + yt_b.wdep())
                yt_b.did_write(tl)
                tso = p.dma("sync", dump_map["a1"][lc * 128:(lc + 1) * 128, :], yt32[:], deps=[tl])
                yt_b.did_read(tso)

        entm = p.fence()
        for e_i in range(8):
            wle = [p.dma("gpsimd", AR_win[:, 0:4, :],
                         win_in[e_i].rearrange("(k p) n -> p k n", p=128)[:, 0:4, :],
                         deps=entm + arena_b.wdep()),
                   p.dma("gpsimd", AR_win[:, 4:8, :],
                         win_in[e_i].rearrange("(k p) n -> p k n", p=128)[:, 4:8, :],
                         deps=entm + arena_b.wdep()),
                   p.dma("gpsimd", AR_wout[:, 0:16, :],
                         wout_in[e_i].rearrange("(c p) n -> p c n", p=128)[:, 0:16, :],
                         deps=entm + arena_b.wdep()),
                   p.dma("gpsimd", AR_wout[:, 16:32, :],
                         wout_in[e_i].rearrange("(c p) n -> p c n", p=128)[:, 16:32, :],
                         deps=entm + arena_b.wdep())]
            for w in wle:
                arena_b.did_write(w)
            for t in range(NT):
                r0 = t * 512
                xg_toks = []
                for tsn in range(4):
                    lc = 4 * t + tsn
                    tl = p.dma("sync", sqa[:], xnb[lc * 128:(lc + 1) * 128, :],
                               deps=sqa_b.wdep() + entm)
                    sqa_b.did_write(tl)
                    tx = p.emit("vector", lambda e, lc=lc, e_i=e_i: e.tensor_scalar_mul(
                        qbt[:], sqa[:], gat[:, lc, e_i:e_i + 1]),
                        deps=[tl, gat_b.w] + qbt_b.wdep())
                    qbt_b.did_write(tx)
                    gat_b.did_read(tx)
                    sqa_b.did_read(tx)
                    tso = p.dma("sync", xgscr[tsn * 128:(tsn + 1) * 128, :], qbt[:], deps=[tx])
                    qbt_b.did_read(tso)
                    xg_toks.append(tso)
                ztl = []
                for k in range(8):
                    tt = p.dmat("scalar", xnf[:, k, :], xgscr[:, k * 128:k * 128 + 128],
                                deps=xg_toks + (xnf_b.wdep() if k == 0 else []))
                    ztl.append(tt)
                xnf_b.did_write(ztl[-1])
                for jc in range(NJ):
                    bank = jc % 2
                    bdeps = bA[bank].wdep() + ztl + wle
                    tm = None
                    for k in range(8):
                        tm = p.emit("tensor", lambda e, jc=jc, k=k, bank=bank: e.matmul(
                            psAv[:, bank, :], AR_win[:, k, jc * 128:jc * 128 + 128], xnf[:, k, :],
                            start=(k == 0), stop=(k == 7)),
                            deps=(bdeps if k == 0 else []), inc=(k == 7))
                    bA[bank].did_write(tm)
                    xnf_b.did_read(tm)
                    arena_b.did_read(tm)
                    te = p.emit("vector", lambda e, jc=jc, bank=bank: e.tensor_copy(
                        hmoe[:, jc, :], psAv[:, bank, :]),
                        deps=[tm] + (hmoe_b.wdep() if jc == 0 else []))
                    hmoe_b.did_write(te)
                    bA[bank].did_read(te)
                h_tok = hmoe_b.w
                for tsn in range(4):
                    lc = 4 * t + tsn
                    for nt in range(2):
                        bank = 2 + (nt % 2)
                        bdeps = bA[bank].wdep() + [h_tok] + wle
                        tm = None
                        for jc in range(NJ):
                            tm = p.emit("tensor", lambda e, jc=jc, tsn=tsn, nt=nt, bank=bank: e.matmul(
                                psAv[:, bank, :], hmoe[:, jc, tsn * 128:tsn * 128 + 128],
                                AR_wout[:, jc, nt * 512:nt * 512 + 512],
                                start=(jc == 0), stop=(jc == NJ - 1)),
                                deps=(bdeps if jc == 0 else []), inc=(jc == NJ - 1))
                        bA[bank].did_write(tm)
                        hmoe_b.did_read(tm)
                        arena_b.did_read(tm)
                        te = p.emit("vector", lambda e, nt=nt, bank=bank: e.tensor_copy(
                            yt32[:, nt * 512:nt * 512 + 512], psAv[:, bank, :]),
                            deps=[tm] + (yt_b.wdep() if nt == 0 else []))
                        yt_b.did_write(te)
                        bA[bank].did_read(te)
                    tso = p.dma("gpsimd", f1acc[lc * 128:(lc + 1) * 128, :], yt32[:],
                                deps=[yt_b.w] + f1_b[t].wdep(), accum_op=ALU.add)
                    yt_b.did_read(tso)
                    f1_b[t].did_write(tso)

        if "f1" in dump_map:
            entg = p.fence()
            for lc in range(NLC):
                tl = p.dma("sync", yt32[:], f1acc[lc * 128:(lc + 1) * 128, :], deps=entg + yt_b.wdep())
                yt_b.did_write(tl)
                tso = p.dma("sync", dump_map["f1"][lc * 128:(lc + 1) * 128, :], yt32[:], deps=[tl])
                yt_b.did_read(tso)

        entf = p.fence()
        for lc in range(NLC):
            tl1 = p.dma("sync", xt32[:], x1_in[lc * 128:(lc + 1) * 128, :], deps=entf + xt32_b.wdep())
            xt32_b.did_write(tl1)
            tl2 = p.dma("sync", yt32[:], f1acc[lc * 128:(lc + 1) * 128, :], deps=entf + yt_b.wdep())
            yt_b.did_write(tl2)
            ta = p.emit("vector", lambda e: e.tensor_add(xt32[:], xt32[:], yt32[:]), deps=[tl1, tl2])
            xt32_b.did_write(ta)
            yt_b.did_read(ta)
            tso = p.dma("sync", out_t[lc * 128:(lc + 1) * 128, :], xt32[:], deps=[ta])
            xt32_b.did_read(tso)

        with nc.Block() as blk:
            p.finalize(blk)
    return nc


def prep_B_shared(ins):
    d = {}
    d["wkT"] = _f32(ins["L1_attn_Wk"].T)
    d["bkT"] = _f32(ins["L1_attn_bk"].reshape(8, 128).T)
    d["bk_r"] = _f32(ins["L1_attn_bk"][None, :])
    d["wvT"] = _f32(ins["L1_attn_Wv"].T)
    d["bv_r"] = _f32(ins["L1_attn_bv"][None, :])
    d["wrf_in"] = _f32(ins["L1_attn_Wrf"][0])
    d["eye_in"] = np.eye(128, dtype=np.float32)
    return d


def prep_C_shared(ins):
    d = {}
    d["wqT"] = _f32(ins["L1_attn_Wq"].T)
    d["bqT"] = _f32(ins["L1_attn_bq"].reshape(8, 128).T)
    d["woT"] = _f32(ins["L1_attn_Wo"].T)
    d["bo_r"] = _f32(ins["L1_attn_bo"][None, :])
    d["wrf_in"] = _f32(ins["L1_attn_Wrf"][0])
    d["wgT"] = _f32(ins["L1_ffn_w_gate"].T)
    d["win_in"] = _f32(np.ascontiguousarray(ins["L1_ffn_w_in"].transpose(0, 2, 1)))
    d["wout_in"] = _f32(np.ascontiguousarray(ins["L1_ffn_w_out"].transpose(0, 2, 1)))
    d["eye_in"] = np.eye(128, dtype=np.float32)
    return d


_CACHE = {}


def kernel(**inputs):
    ins = inputs
    if "A" not in _CACHE:
        _CACHE["A"] = build_A()
    shared = prep_A_shared(ins)
    shared_clean = {k: v for k, v in shared.items() if not k.startswith("_")}
    in_maps = []
    for core in range(8):
        b, h = core // 2, core % 2
        m = dict(shared_clean)
        m.update(prep_A_core(ins, shared, b, h))
        in_maps.append(m)
    resA = run_bass_kernel_spmd(_CACHE["A"], in_maps, core_ids=list(range(8)))
    x1s = [np.asarray(r["x1"]) for r in resA.results]

    if "B" not in _CACHE:
        _CACHE["B"] = build_B()
    sB = prep_B_shared(ins)
    in_maps = []
    for core in range(8):
        m = dict(sB)
        m["x1_in"] = x1s[core]
        in_maps.append(m)
    resB = run_bass_kernel_spmd(_CACHE["B"], in_maps, core_ids=list(range(8)))
    stats = [np.asarray(r["stats"]) for r in resB.results]

    if "C" not in _CACHE:
        _CACHE["C"] = build_C()
    sC = prep_C_shared(ins)
    in_maps = []
    for core in range(8):
        b, h = core // 2, core % 2
        m = dict(sC)
        m["x1_in"] = x1s[core]
        m["stats_in"] = np.ascontiguousarray(
            np.stack([stats[2 * b], stats[2 * b + 1]]))
        in_maps.append(m)
    resC = run_bass_kernel_spmd(_CACHE["C"], in_maps, core_ids=list(range(8)))
    outs = [np.asarray(r["out"]) for r in resC.results]

    full = np.zeros((4, 8192, D), dtype=np.float32)
    for core in range(8):
        b, h = core // 2, core % 2
        full[b, h * REALT:(h + 1) * REALT, :] = outs[core]
    return full
